# version 29
# baseline (speedup 1.0000x reference)
import numpy as np
import ml_dtypes
import concourse.bass as bass
import concourse.mybir as mybir
from concourse.bass_utils import run_bass_kernel_spmd
from contextlib import ExitStack

BFD = ml_dtypes.bfloat16
F32 = mybir.dt.float32
BF16 = mybir.dt.bfloat16
ALU = mybir.AluOpType
AF = mybir.ActivationFunctionType

T = 2048
D = 1024
R = 64
NT = 16
NS = 4
KC = 8
CW = 2052


class Sem:
    def __init__(self, nc, st, name):
        self.h = st.enter_context(nc.semaphore(name))
        self.n = 0

    def inc(self, inst, k=1):
        inst.then_inc(self.h, k)
        self.n += k
        return self.n


def build_nc(stage=5):
    nc = bass.Bass("TRN2", target_bir_lowering=False, detect_race_conditions=False)
    dt = nc.dram_tensor
    x_d = dt("x", [T, D], F32, kind="ExternalInput").ap()
    wq_d = dt("wq", [D, D], F32, kind="ExternalInput").ap()
    wk_d = dt("wk", [D, D], F32, kind="ExternalInput").ap()
    winv_d = dt("winv", [D, D], F32, kind="ExternalInput").ap()
    woT_d = dt("woT", [D, D], F32, kind="ExternalInput").ap()
    wrT_d = dt("wrT", [D, D], F32, kind="ExternalInput").ap()
    wkT_d = dt("wkT", [D, D], F32, kind="ExternalInput").ap()
    wvT_d = dt("wvT", [D, D], F32, kind="ExternalInput").ap()
    vb_d = dt("vb", [D, R], F32, kind="ExternalInput").ap()
    wb_d = dt("wb", [D, R], F32, kind="ExternalInput").ap()
    vt_d = dt("vt", [D, R], F32, kind="ExternalInput").ap()
    wt_d = dt("wt", [D, R], F32, kind="ExternalInput").ap()
    xt_d = dt("xt", [D, R], F32, kind="ExternalInput").ap()
    ub_d = dt("ub", [D, R], F32, kind="ExternalInput").ap()
    ut_d = dt("ut", [D, R], F32, kind="ExternalInput").ap()
    biasv_d = dt("biasv", [128, 16], F32, kind="ExternalInput").ap()
    mus_d = dt("mus", [128, 16], F32, kind="ExternalInput").ap()
    ab_d = dt("ab", [128, 2], F32, kind="ExternalInput").ap()
    ident_d = dt("ident", [128, 128], BF16, kind="ExternalInput").ap()
    icbc_d = dt("icbc", [64, T], F32, kind="ExternalInput").ap()
    ictile_d = dt("ictile", [128, NT], F32, kind="ExternalInput").ap()
    out_d = dt("out", [T, D], F32, kind="ExternalOutput").ap()

    st = ExitStack()
    sb = lambda n, s, d_: st.enter_context(nc.sbuf_tensor(n, s, d_))
    xnf = sb("xnf", [128, KC * CW], BF16)
    SA = sb("SA", [128, T], BF16)
    SB = sb("SB", [128, T], BF16)
    SC = sb("SC", [128, T], BF16)
    SD = sb("SD", [128, T], BF16)
    Pp = sb("Pp", [128, KC * 448], BF16)
    Pg = sb("Pg", [128, D], BF16)
    Pgt = sb("Pgt", [128, D], BF16)
    c0b = sb("c0b", [128, D], F32)
    c0row = sb("c0row", [1, D], BF16)
    icbc = sb("icbc_s", [128, T], F32)
    id_sb = sb("id_sb", [128, 128], BF16)
    ones1 = sb("ones1", [128, 1], BF16)
    onesr = sb("onesr", [1, 128], BF16)
    ictile = sb("ictile_s", [128, NT], F32)
    mus = sb("mus_s", [128, 16], F32)
    biasv = sb("biasv_s", [128, 16], F32)
    ab = sb("ab_s", [128, 2], F32)
    bcq = sb("bcq", [128, 8], BF16)
    bc32 = sb("bc32", [128, 8], F32)
    bct = sb("bct", [128, 8], F32)
    wrT = sb("wrT_s", [128, KC * D], BF16)
    wkT = sb("wkT_s", [128, KC * D], BF16)
    wvT = sb("wvT_s", [128, KC * D], BF16)
    xin = [sb(f"xin{i}", [128, D], F32) for i in range(4)]
    xntm = [sb(f"xntm{i}", [128, D], BF16) for i in range(2)]
    s1 = sb("s1", [128, NT], F32)
    s2 = sb("s2", [128, NT], F32)
    mean = sb("mean", [128, NT], F32)
    m2 = sb("m2", [128, NT], F32)
    var = sb("var", [128, NT], F32)
    rstdv = sb("rstdv", [128, NT], F32)
    eps1 = sb("eps1", [128, 1], F32)
    invd = sb("invd", [128, 1], F32)
    nhalf = sb("nhalf", [128, 1], F32)
    lnv = sb("lnv", [128, NT], F32)

    xnf_r = xnf[:, :].rearrange("p (k c) -> p k c", c=CW)
    Pp_r = Pp[:, :].rearrange("p (k c) -> p k c", c=448)

    sem = lambda n: Sem(nc, st, n)
    d_cst = sem("d_cst")
    d_sml = sem("d_sml")
    d_vb = sem("d_vb")
    d_x = [sem("d_x0"), sem("d_x1")]
    d_pw = sem("d_pw")
    d_wv1 = sem("d_wv1")
    d_fw1 = sem("d_fw1")
    d_fw2 = sem("d_fw2")
    d_fw3 = sem("d_fw3")
    d_a7 = sem("d_a7")
    d_icb = sem("d_icb")
    d_xre = [sem("d_xre0"), sem("d_xre1")]
    d_out = [sem("d_out0"), sem("d_out1")]
    c_sq = sem("c_sq")
    c_var = sem("c_var")
    c_sd = sem("c_sd")
    c_xn = sem("c_xn")
    c_tp = sem("c_tp")
    c_xnf = sem("c_xnf")
    c_pc = sem("c_pc")
    c_pce = sem("c_pce")
    c_bcq = sem("c_bcq")
    c_pre = sem("c_pre")
    c_proj = sem("c_proj")
    c_proje = sem("c_proje")
    c_scan4 = sem("c_scan4")
    c_tri = sem("c_tri")
    c_xrk = sem("c_xrk")
    c_k = sem("c_k")
    c_kr = sem("c_kr")
    c_ksq = sem("c_ksq")
    c_r = sem("c_r")
    c_sig = sem("c_sig")
    c_v = sem("c_v")
    c_at = sem("c_at")
    c_e12 = sem("c_e12")
    c_pool = sem("c_pool")
    c_bc0 = sem("c_bc0")

    c_ms = sem("c_ms")
    nc.gpsimd.memset(ones1[:, :], 1.0)
    nc.gpsimd.memset(onesr[:, :], 1.0)
    for _k in range(KC):
        nc.gpsimd.memset(xnf[:, CW * _k:CW * _k + 2], 0.0)
        nc.gpsimd.memset(xnf[:, CW * _k + 2050:CW * (_k + 1)], 0.0)
    nc.gpsimd.memset(eps1[:, :], 1e-5)
    nc.gpsimd.memset(invd[:, :], 1.0 / D)
    c_ms.inc(nc.gpsimd.memset(nhalf[:, :], -1.0))
    g = nc.gpsimd

    x_tiles_loaded = []
    def _xdma(t):
        if t >= 4:
            nc.sync.wait_ge(c_xn.h, t - 3)
        d_x[t % 2].inc(nc.sync.dma_start(xin[t % 4][:, :],
                                         x_d[128 * t:128 * (t + 1), :]), 16)
        x_tiles_loaded.append(d_x[t % 2].n)
    _xdma(0)
    _xdma(1)
    d_cst.inc(nc.sync.dma_start(id_sb[:, :], ident_d), 16)
    d_cst.inc(nc.sync.dma_start(ictile[:, :], ictile_d), 16)
    d_cst.inc(nc.sync.dma_start(mus[:, :], mus_d), 16)
    d_cst.inc(nc.sync.dma_start(biasv[:, :], biasv_d), 16)
    d_cst.inc(nc.sync.dma_start(ab[:, :], ab_d), 16)
    for t in range(2, NT):
        _xdma(t)
    d_icb.inc(nc.sync.dma_start(icbc[0:64, :], icbc_d), 16)

    st1o = ExitStack()
    sb1 = lambda n, s, d_: st1o.enter_context(nc.sbuf_tensor(n, s, d_))
    PAr = sb1("PAr", [128, KC * 192], BF16)
    PBr = sb1("PBr", [128, KC * 192], BF16)
    sqd = sb1("sqd", [128, D], BF16)
    nc.gpsimd.memset(sqd[:, :], 0.0)
    xcb = sb1("xcb", [128, D], F32)
    UU = sb1("UU", [128, KC * 128], BF16)
    uts = sb1("uts", [128, KC * 64], BF16)
    st1b = ExitStack()
    wq_s = st1b.enter_context(nc.sbuf_tensor("wq_s", [128, KC * D], BF16))
    wk_s = st1b.enter_context(nc.sbuf_tensor("wk_s", [128, KC * D], BF16))
    st1a = ExitStack()
    winv_s = st1a.enter_context(nc.sbuf_tensor("winv_s", [128, KC * D], BF16))
    vbwb = st1a.enter_context(nc.sbuf_tensor("vbwb", [128, KC * 128], BF16))
    PAr_r = PAr[:, :].rearrange("p (k c) -> p k c", c=192)
    PBr_r = PBr[:, :].rearrange("p (k c) -> p k c", c=192)
    UU_r = UU[:, :].rearrange("p (k c) -> p k c", c=128)
    uts_r = uts[:, :].rearrange("p (k c) -> p k c", c=64)
    vbwb_r = vbwb[:, :].rearrange("p (k c) -> p k c", c=128)

    def chunked(dr):
        return dr.rearrange("(k p) m -> p k m", p=128)

    d_vb.inc(g.dma_start(vbwb_r[:, :, 0:64], chunked(vb_d)), 16)
    d_vb.inc(g.dma_start(vbwb_r[:, :, 64:128], chunked(wb_d)), 16)
    winv_r0 = winv_s[:, :].rearrange("p (k c) -> p k c", c=D)
    d_wv1.inc(g.dma_start(winv_r0[:, :, 0:512], chunked(winv_d)[:, :, 0:512]), 16)
    d_pw.inc(g.dma_start(winv_r0[:, :, 512:D], chunked(winv_d)[:, :, 512:D]), 16)
    d_pw.inc(g.dma_start(wq_s[:, :].rearrange("p (k c) -> p k c", c=D),
                         chunked(wq_d)), 16)
    d_pw.inc(g.dma_start(wk_s[:, :].rearrange("p (k c) -> p k c", c=D),
                         chunked(wk_d)), 16)
    d_sml.inc(g.dma_start(PAr_r[:, :, 0:64], chunked(vb_d)), 16)
    d_sml.inc(g.dma_start(PAr_r[:, :, 128:192], chunked(vt_d)), 16)
    d_sml.inc(g.dma_start(PBr_r[:, :, 0:64], chunked(wb_d)), 16)
    d_sml.inc(g.dma_start(PBr_r[:, :, 128:192], chunked(wt_d)), 16)
    d_sml.inc(g.dma_start(Pp_r[:, :, 320:384], chunked(xt_d)), 16)
    d_sml.inc(g.dma_start(UU_r[:, :, 0:64], chunked(ub_d)), 16)
    d_sml.inc(g.dma_start(UU_r[:, :, 64:128], chunked(ub_d)), 16)
    d_sml.inc(g.dma_start(uts_r[:, :, :], chunked(ut_d)), 16)

    psA = ExitStack()
    psTP = [psA.enter_context(nc.psum_tensor(f"psTP{i}", [128, 1024], BF16)) for i in range(3)]
    psPC = [psA.enter_context(nc.psum_tensor(f"psPC{i}", [128, 512], F32)) for i in range(2)]
    psPG = psA.enter_context(nc.psum_tensor("psPG", [128, 1024], F32))

    nc.vector.wait_ge(d_cst.h, d_cst.n)
    nc.vector.scalar_tensor_tensor(bc32[:, :], biasv[:, 0:8], ab[:, 0:1], biasv[:, 0:8],
                                   ALU.mult, ALU.add)
    c_bc0.inc(nc.vector.tensor_scalar(bct[:, :], biasv[:, 8:16], ab[:, 1:2], None, ALU.mult))

    def emit_dve_ln_iter(it, extra=None):
        if it < NT:
            nc.vector.wait_ge(d_x[it % 2].h, x_tiles_loaded[it])
            nc.vector.reduce_sum(s1[:, it:it + 1], xin[it % 4][:, :],
                                 axis=mybir.AxisListType.X)
        elif it == NT:
            nc.vector.tensor_copy(sqd[:, 0:256], sqd[:, 0:256])
        if 1 <= it <= NT:
            t = it - 1
            nc.vector.wait_ge(c_sd.h, t + 1)
            nc.vector.reciprocal(rstdv[:, t:t + 1], lnv[:, t:t + 1])
        if 2 <= it <= NT + 1:
            t = it - 2
            nc.vector.wait_ge(c_tp.h, t + 1)
            srcv = psTP[t % 3][:, :].rearrange("p (k c) -> p k c", c=128)
            nc.vector.tensor_copy(
                xnf_r[:, :, 2 + 128 * t:2 + 128 * t + 64],
                srcv[:, :, 0:64])
        else:
            nc.vector.tensor_copy(sqd[:, 0:256], sqd[:, 0:256])
        if 1 <= it <= NT:
            t = it - 1
            c_xn.inc(nc.vector.tensor_scalar(xntm[t % 2][:, :], xin[t % 4][:, :],
                                             mean[:, t:t + 1], rstdv[:, t:t + 1],
                                             ALU.subtract, ALU.mult))
        else:
            nc.vector.tensor_copy(sqd[:, 0:256], sqd[:, 0:256])
        if it < NT:
            nc.vector.tensor_scalar(mean[:, it:it + 1], s1[:, it:it + 1],
                                    invd[:, 0:1], None, ALU.mult)
        if 2 <= it <= NT + 1:
            t = it - 2
            srcv = psTP[t % 3][:, :].rearrange("p (k c) -> p k c", c=128)
            c_xnf.inc(nc.vector.tensor_copy(
                xnf_r[:, :, 2 + 128 * t + 64:2 + 128 * (t + 1)],
                srcv[:, :, 64:128]))
        else:
            nc.vector.tensor_copy(sqd[:, 0:256], sqd[:, 0:256])
        if it < NT:
            c_var.inc(nc.vector.tensor_scalar(m2[:, it:it + 1], mean[:, it:it + 1],
                                              mean[:, it:it + 1], nhalf[:, 0:1],
                                              ALU.mult, ALU.mult))
        if extra is not None:
            extra()

    def emit_act_ln_iter(it, extra=None):
        if 1 <= it <= NT:
            t = it - 1
            nc.scalar.wait_ge(c_var.h, t + 1)
            nc.scalar.activation(var[:, t:t + 1], s2[:, t:t + 1], AF.Identity,
                                 scale=invd[:, 0:1], bias=m2[:, t:t + 1])
            c_sd.inc(nc.scalar.activation(lnv[:, t:t + 1], var[:, t:t + 1], AF.Sqrt,
                                          bias=eps1[:, 0:1]))
        if it < NT:
            nc.scalar.wait_ge(d_x[it % 2].h, x_tiles_loaded[it])
            if it == 0:
                nc.scalar.wait_ge(c_ms.h, 1)
            nc.scalar.activation(sqd[:, :], xin[it % 4][:, :], AF.Square,
                                 accum_out=s2[:, it:it + 1])
        if extra is not None:
            extra()

    def emit_pe_tp(t):
        if t == 0:
            nc.tensor.wait_ge(d_cst.h, d_cst.n)
            nc.tensor.wait_ge(c_ms.h, 1)
        nc.tensor.wait_ge(c_xn.h, t + 1)
        if t >= 3:
            nc.tensor.wait_ge(c_xnf.h, t - 2)
        for k in range(KC):
            inst = nc.tensor.transpose(psTP[t % 3][:, 128 * k:128 * (k + 1)],
                                       xntm[t % 2][:, 128 * k:128 * (k + 1)], id_sb[:, :])
            if k == KC - 1:
                c_tp.inc(inst)

    winv_r = winv_s[:, :].rearrange("p (k c) -> p k c", c=D)
    wqr_box = {}

    def emit_tmp12(m):
        if m == 0:
            nc.tensor.wait_ge(d_wv1.h, 16)
            nc.tensor.wait_ge(d_vb.h, 32)
        if m == 4:
            nc.tensor.wait_ge(d_pw.h, 16)
        if m >= 2:
            nc.tensor.wait_ge(c_pce.h, 2 * (m - 1))
        for k in range(KC):
            inst = nc.tensor.matmul(psPC[m % 2][:, 0:128],
                                    winv_r[:, k, 128 * m:128 * (m + 1)],
                                    vbwb[:, 128 * k:128 * (k + 1)],
                                    start=(k == 0), stop=(k == KC - 1))
        c_pc.inc(inst)

    def emit_pa(m):
        if m == 0:
            nc.tensor.wait_ge(d_pw.h, 32)
            nc.tensor.wait_ge(d_sml.h, 128)
            nc.tensor.wait_ge(c_pce.h, 16)
        if m >= 2:
            nc.tensor.wait_ge(c_pce.h, 16 + 2 * (m - 1))
        for k in range(KC):
            inst = nc.tensor.matmul(psPC[m % 2][:, 0:192],
                                    wqr_box["wq"][:, k, 128 * m:128 * (m + 1)],
                                    PAr[:, 192 * k:192 * (k + 1)],
                                    start=(k == 0), stop=(k == KC - 1))
        c_pc.inc(inst)

    def emit_pb(m):
        if m == 0:
            nc.tensor.wait_ge(d_pw.h, 48)
        nc.tensor.wait_ge(c_pce.h, 30 + 2 * m)
        for k in range(KC):
            inst = nc.tensor.matmul(psPC[m % 2][:, 0:192],
                                    wqr_box["wk"][:, k, 128 * m:128 * (m + 1)],
                                    PBr[:, 192 * k:192 * (k + 1)],
                                    start=(k == 0), stop=(k == KC - 1))
        c_pc.inc(inst)

    pe_slots = {2: [0, 1], 3: [2, 3], 4: [4, 5], 5: [6, 7],
                11: [8, 9], 12: [10, 11], 13: [12, 13], 14: [14, 15],
                15: [16, 17]}
    pre_units = ([lambda m=m: emit_tmp12(m) for m in range(KC)]
                 + [lambda m=m: emit_pa(m) for m in range(KC)]
                 + [lambda m=m: emit_pb(m) for m in range(KC)])

    for t in range(NT):
        emit_pe_tp(t)
        if stage >= 2:
            for u in pe_slots.get(t, []):
                pre_units[u]()
            if t == 8:
                st1a.close()
                woT_s = st1b.enter_context(nc.sbuf_tensor("woT_s", [128, KC * D], BF16))
                g.wait_ge(c_pc.h, 8)
                d_pw.inc(g.dma_start(woT_s[:, :].rearrange("p (k c) -> p k c", c=D),
                                     woT_d.rearrange("(k p) m -> p k m", p=128)), 16)
                g.wait_ge(d_x[0].h, 16 * (NT // 2))
                g.wait_ge(d_x[1].h, 16 * (NT // 2))
                d_fw1.inc(g.dma_start(wkT[:, :].rearrange("p (k c) -> p k c", c=D),
                                      wkT_d.rearrange("(k p) m -> p k m", p=128)), 16)
                d_fw2.inc(g.dma_start(wrT[:, :].rearrange("p (k c) -> p k c", c=D),
                                      wrT_d.rearrange("(k p) m -> p k m", p=128)), 16)
                d_fw3.inc(g.dma_start(wvT[:, :].rearrange("p (k c) -> p k c", c=D),
                                      wvT_d.rearrange("(k p) m -> p k m", p=128)), 16)
                wqr_box["wq"] = wq_s[:, :].rearrange("p (k c) -> p k c", c=D)
                wqr_box["wk"] = wk_s[:, :].rearrange("p (k c) -> p k c", c=D)
                wqr_box["woT"] = woT_s[:, :].rearrange("p (k c) -> p k c", c=D)

    if stage < 2:
        for it in range(NT + 2):
            emit_dve_ln_iter(it)
        _dbg_finish(nc, g, stage, locals())
        st1a.close(); st1b.close(); st1o.close(); psA.close(); st.close()
        return nc
    for u in range(18, 24):
        pre_units[u]()

    woT_r = wqr_box["woT"]
    nc.tensor.wait_ge(d_pw.h, 64)
    nc.tensor.wait_ge(c_bcq.h, 1)
    for h in range(2):
        for k in range(KC):
            inst = nc.tensor.matmul(psPG[0:1, 512 * h:512 * (h + 1)],
                                    bcq[:, k:k + 1], woT_r[:, k, 512 * h:512 * (h + 1)],
                                    start=(k == 0), stop=(k == KC - 1))
    c_pc.inc(inst)
    nc.tensor.wait_ge(c_pce.h, 49)
    for h in range(2):
        for k in range(KC):
            inst = nc.tensor.matmul(psPG[:, 512 * h:512 * (h + 1)],
                                    UU[:, 128 * k:128 * (k + 1)],
                                    woT_r[:, k, 512 * h:512 * (h + 1)],
                                    start=(k == 0), stop=(k == KC - 1))
    c_pc.inc(inst)
    for h in range(2):
        inst = nc.tensor.matmul(psPC[h][:, :], onesr[:, :],
                                c0row[:, 512 * h:512 * (h + 1)], start=True, stop=True)
    c_pc.inc(inst)
    nc.tensor.wait_ge(c_pce.h, 50)
    for h in range(2):
        for k in range(KC):
            inst = nc.tensor.matmul(psPG[0:64, 512 * h:512 * (h + 1)],
                                    uts[:, 64 * k:64 * (k + 1)],
                                    woT_r[:, k, 512 * h:512 * (h + 1)],
                                    start=(k == 0), stop=(k == KC - 1))
    c_pc.inc(inst)

    def act_tmp12_evac(m):
        nc.scalar.wait_ge(c_pc.h, m + 1)
        c_pce.inc(nc.scalar.activation(PAr_r[:, m, 64:128], psPC[m % 2][:, 64:128],
                                       AF.Identity, scale=ab[:, 0:1]))
        c_pce.inc(nc.scalar.copy(PBr_r[:, m, 64:128], psPC[m % 2][:, 0:64]))

    def act_pa_evac(m):
        nc.scalar.wait_ge(c_pc.h, 9 + m)
        c_pce.inc(nc.scalar.copy(Pp_r[:, m, 0:128], psPC[m % 2][:, 0:128]))
        c_pce.inc(nc.scalar.copy(Pp_r[:, m, 256:320], psPC[m % 2][:, 128:192]))

    def act_pb_evac(m):
        nc.scalar.wait_ge(c_pc.h, 17 + m)
        c_pce.inc(nc.scalar.copy(Pp_r[:, m, 128:256], psPC[m % 2][:, 0:128]))
        c_pce.inc(nc.scalar.copy(Pp_r[:, m, 384:448], psPC[m % 2][:, 128:192]))

    act_slots = {4: [0, 1], 5: [2, 3], 6: [4, 5], 7: [6, 7],
                 14: [8, 9], 15: [10, 11], 16: [12, 13], 17: [14, 15]}
    act_units = ([lambda m=m: act_tmp12_evac(m) for m in range(KC)]
                 + [lambda m=m: act_pa_evac(m) for m in range(KC)]
                 + [lambda m=m: act_pb_evac(m) for m in range(KC)])

    for it in range(NT + 2):
        def _extra(it=it):
            for u in act_slots.get(it, []):
                act_units[u]()
        emit_act_ln_iter(it, _extra)
    for u in range(16, 24):
        act_units[u]()
    nc.scalar.wait_ge(c_pc.h, 25)
    c_pce.inc(nc.scalar.copy(c0row[:, :], psPG[0:1, :]))
    nc.scalar.wait_ge(c_pc.h, 26)
    c_pce.inc(nc.scalar.copy(Pg[:, :], psPG[:, :]))
    nc.scalar.wait_ge(c_pc.h, 27)
    c_pce.inc(nc.scalar.copy(c0b[:, 0:512], psPC[0][:, :]))
    c_pce.inc(nc.scalar.copy(c0b[:, 512:1024], psPC[1][:, :]))
    nc.scalar.wait_ge(c_pc.h, 28)
    c_pce.inc(nc.scalar.activation(Pgt[0:64, :], psPG[0:64, :], AF.Identity,
                                   scale=ab[0:64, 1:2]))

    def _dve_extra(it):
        if it == 2:
            nc.vector.wait_ge(c_bc0.h, 1)
            nc.vector.tensor_tensor(bc32[:, :], bc32[:, :], bct[:, :], ALU.add)
        elif it == 4:
            c_bcq.inc(nc.vector.tensor_copy(bcq[:, :], bc32[:, :]))
    for it in range(NT + 2):
        emit_dve_ln_iter(it, lambda it=it: _dve_extra(it))

    st1b.close()
    st1o.close()
    psA.close()

    st2 = ExitStack()
    sb2 = lambda n, s, d_: st2.enter_context(nc.sbuf_tensor(n, s, d_))
    xrb = [sb2(f"xrb{i}", [128, KC * 512], BF16) for i in range(2)]
    xkb = [sb2(f"xkb{i}", [128, KC * 512], BF16) for i in range(2)]
    kact = sb2("kact", [128, KC * 512], BF16)
    dbuf = sb2("dbuf", [128, 512], BF16)
    kr = [sb2(f"kr{i}", [128, 512], BF16) for i in range(4)]
    r_tm = [sb2(f"r_tm{i}", [128, D], BF16) for i in range(2)]
    e1 = sb2("e1", [128, D], F32)
    e2 = sb2("e2", [128, D], F32)
    xre = [sb2(f"xre{i}", [128, D], F32) for i in range(2)]

    if stage < 3:
        _dbg_finish(nc, g, stage, locals())
        st2.close(); st.close()
        return nc
    psVA = ExitStack()
    psV = psVA.enter_context(nc.psum_tensor("psV", [128, 1024], F32))
    psAT = psVA.enter_context(nc.psum_tensor("psAT", [128, 1024], F32))
    psB = ExitStack()
    pj = [psB.enter_context(nc.psum_tensor(f"pj{p}", [128, 512], F32))
          for p in range(4)]

    nc.tensor.wait_ge(c_xnf.h, NT)
    nc.tensor.wait_ge(c_pce.h, 53)
    proj_groups = []
    for s in range(NS):
        for p in range(4):
            if s >= 1:
                nc.tensor.wait_ge(c_proje.h, 4 * (s - 1) + p + 1)
            dst = pj[p][0:64, :] if p == 3 else pj[p][:, :]
            for k in range(KC):
                lcol = 448 * k + (384 if p == 3 else 128 * p)
                lw = 64 if p == 3 else 128
                inst = nc.tensor.matmul(dst, Pp[:, lcol:lcol + lw],
                                        xnf_r[:, k, 2 + 512 * s:2 + 512 * (s + 1)],
                                        start=(k == 0), stop=(k == KC - 1))
            c_proj.inc(inst)
            proj_groups.append((s, p, c_proj.n))

    evac_count = {}
    _ec = 0
    for s, p, pn in proj_groups:
        nc.scalar.wait_ge(c_proj.h, pn)
        if p == 0 or p == 1:
            dst = SA if p == 0 else SB
            c_proje.inc(nc.scalar.copy(dst[:, 512 * s:512 * (s + 1)],
                                       pj[p][:, :]))
            _ec += 1
        elif p == 2:
            nc.scalar.copy(SC[0:64, 512 * s:512 * (s + 1)], pj[2][0:64, :])
            c_proje.inc(nc.scalar.copy(SD[64:128, 512 * s:512 * (s + 1)],
                                       pj[2][64:128, :]))
            c_proje.n += 0
            _ec += 1
            c_proje.inc(nc.scalar.engine_nop()) if False else None
            _ec += 0
        else:
            c_proje.inc(nc.scalar.copy(SD[0:64, 512 * s:512 * (s + 1)],
                                       pj[3][0:64, :]))
            _ec += 1
        evac_count[(s, p)] = c_proje.n

    if stage < 4:
        _dbg_finish(nc, g, stage, locals())
        psB.close(); psVA.close(); st2.close(); st.close()
        return nc
    psB.close()
    psC = ExitStack()
    psK = [psC.enter_context(nc.psum_tensor(f"psK{i}", [128, 512], F32)) for i in range(2)]
    psR = psC.enter_context(nc.psum_tensor("psR", [128, 1024], F32))

    def emit_xrk(s, wait_r=None):
        nc.vector.wait_ge(c_pc.h, 28)
        if wait_r is not None:
            nc.vector.wait_ge(c_r.h, wait_r)
        for c in range(KC):
            xc = xnf_r[:, c, 2 + 512 * s:2 + 512 * (s + 1)]
            xp = xnf_r[:, c, 1 + 512 * s:1 + 512 * (s + 1)]
            nc.vector.tensor_tensor(dbuf[:, :], xc, xp, ALU.subtract)
            nc.vector.scalar_tensor_tensor(xrb[s % 2][:, 512 * c:512 * (c + 1)],
                                           dbuf[:, :], mus[:, c:c + 1], xp,
                                           ALU.mult, ALU.add)
            inst = nc.vector.scalar_tensor_tensor(xkb[s % 2][:, 512 * c:512 * (c + 1)],
                                                  dbuf[:, :], mus[:, 8 + c:9 + c], xp,
                                                  ALU.mult, ALU.add)
        c_xrk.inc(inst)

    emit_xrk(0)

    H = T // 2
    ones_h = ones1[:, 0:1].broadcast_to((128, H))
    nc.vector.wait_ge(c_ms.h, 1)
    nc.vector.wait_ge(c_proje.h, 4 * 1 + 2)
    nc.vector.tensor_tensor_scan(psV[:, :], ones_h, SB[:, 0:H], 0.0, ALU.mult, ALU.add)
    nc.vector.tensor_tensor(SB[:, 0:H], SA[:, 0:H], psV[:, :], ALU.mult)
    nc.vector.wait_ge(c_proje.h, 4 * 3 + 2)
    nc.vector.tensor_tensor_scan(psAT[:, :], ones_h, SB[:, H:T],
                                 psV[:, H - 1:H], ALU.mult, ALU.add)
    nc.vector.tensor_tensor(SB[:, H:T], SA[:, H:T], psAT[:, :], ALU.mult)
    nc.vector.wait_ge(c_proje.h, 4 * 1 + 4)
    nc.vector.tensor_tensor_scan(psV[:, :], ones_h, SD[:, 0:H], 0.0, ALU.mult, ALU.add)
    nc.vector.tensor_copy(SC[64:128, 0:H], psV[64:128, :])
    nc.vector.wait_ge(c_proje.h, 16)
    nc.vector.tensor_tensor_scan(psAT[:, :], ones_h, SD[:, H:T],
                                 psV[:, H - 1:H], ALU.mult, ALU.add)
    c_scan4.inc(nc.vector.tensor_copy(SC[64:128, H:T], psAT[64:128, :]))
    g.wait_ge(c_scan4.h, 1)
    d_a7.inc(g.dma_start(SA[0:64, :], SC[64:128, :]), 16)
    nc.vector.tensor_tensor(SD[0:64, 0:H], SC[0:64, 0:H], psV[0:64, :], ALU.mult)
    nc.vector.tensor_tensor(SD[0:64, H:T], SC[0:64, H:T], psAT[0:64, :], ALU.mult)
    nc.vector.wait_ge(d_a7.h, 16)
    nc.vector.wait_ge(d_icb.h, 16)
    nc.vector.tensor_tensor(SC[0:64, :], SA[0:64, :], icbc[0:64, :], ALU.mult)
    c_tri.inc(nc.vector.tensor_tensor(SD[0:64, :], SD[0:64, :], SC[0:64, :], ALU.mult))
    emit_xrk(1)

    if stage < 5:
        _dbg_finish(nc, g, stage, locals())
        psC.close(); psVA.close(); st2.close(); st.close()
        return nc
    nc.tensor.wait_ge(c_proje.h, 16)
    nc.tensor.wait_ge(d_fw1.h, 16)
    for s in range(NS):
        nc.tensor.wait_ge(c_xrk.h, s + 1)
        for d8 in range(KC):
            gi = KC * s + d8
            if gi >= 2:
                nc.tensor.wait_ge(c_kr.h, gi - 1)
            for c in range(KC):
                inst = nc.tensor.matmul(psK[d8 % 2][:, :],
                                        wkT[:, 1024 * c + 128 * d8:1024 * c + 128 * (d8 + 1)],
                                        xkb[s % 2][:, 512 * c:512 * (c + 1)],
                                        start=(c == 0), stop=(c == KC - 1))
            c_k.inc(inst)
        if s == 0:
            nc.tensor.wait_ge(d_fw2.h, 16)
            nc.tensor.wait_ge(d_fw3.h, 16)
        for j in range(4):
            t = 4 * s + j
            if t >= 1:
                nc.tensor.wait_ge(c_sig.h, t)
            for c in range(KC):
                lhsT = xrb[s % 2][:, 512 * c + 128 * j:512 * c + 128 * (j + 1)]
                nc.tensor.matmul(psR[:, 0:512], lhsT, wrT[:, 1024 * c:1024 * c + 512],
                                 start=(c == 0), stop=(c == KC - 1))
                inst = nc.tensor.matmul(psR[:, 512:1024], lhsT,
                                        wrT[:, 1024 * c + 512:1024 * (c + 1)],
                                        start=(c == 0), stop=(c == KC - 1))
            c_r.inc(inst)
            if s == 0 and j == 0:
                nc.tensor.wait_ge(c_tri.h, 1)
            if j == 0:
                nc.tensor.wait_ge(c_ksq.h, KC * (s + 1))
            if t >= 1:
                nc.tensor.wait_ge(c_e12.h, 2 * t)
            for d8 in range(KC):
                lhsT = kact[:, 512 * d8 + 128 * j:512 * d8 + 128 * (j + 1)]
                nc.tensor.matmul(psV[:, 0:512], lhsT, wvT[:, 1024 * d8:1024 * d8 + 512],
                                 start=(d8 == 0), stop=(d8 == KC - 1))
                inst = nc.tensor.matmul(psV[:, 512:1024], lhsT,
                                        wvT[:, 1024 * d8 + 512:1024 * (d8 + 1)],
                                        start=(d8 == 0), stop=(d8 == KC - 1))
            c_v.inc(inst)
            for h in range(2):
                nc.tensor.matmul(psAT[:, 512 * h:512 * (h + 1)],
                                 SB[:, 128 * t:128 * (t + 1)],
                                 Pg[:, 512 * h:512 * (h + 1)], start=True, stop=False)
                inst = nc.tensor.matmul(psAT[:, 512 * h:512 * (h + 1)],
                                        SD[0:64, 128 * t:128 * (t + 1)],
                                        Pgt[0:64, 512 * h:512 * (h + 1)],
                                        start=False, stop=True)
            c_at.inc(inst)

    for s in range(NS):
        for d8 in range(KC):
            gi = KC * s + d8
            nc.scalar.wait_ge(c_k.h, gi + 1)
            c_kr.inc(nc.scalar.activation(kr[gi % 4][:, :], psK[d8 % 2][:, :], AF.Relu))
            c_ksq.inc(nc.scalar.activation(kact[:, 512 * d8:512 * (d8 + 1)],
                                           kr[gi % 4][:, :], AF.Square))
        for j in range(4):
            t = 4 * s + j
            nc.scalar.wait_ge(c_r.h, t + 1)
            if t >= 2:
                nc.scalar.wait_ge(c_e12.h, 2 * t - 2)
            c_sig.inc(nc.scalar.activation(r_tm[j % 2][:, :], psR[:, :], AF.Sigmoid))

    xre_loaded_box = {t: 16 * (t // 2 + 1) for t in range(NT)}
    def emit_e(s, j):
        t = 4 * s + j
        nc.vector.wait_ge(c_v.h, t + 1)
        nc.vector.wait_ge(c_sig.h, t + 1)
        c_e12.inc(nc.vector.tensor_tensor(e2[:, :], r_tm[j % 2][:, :], psV[:, :],
                                          ALU.mult))
        nc.vector.wait_ge(c_at.h, t + 1)
        c_e12.inc(nc.vector.scalar_tensor_tensor(e1[:, :], psAT[:, :],
                                                 ictile[:, t:t + 1], c0b[:, :],
                                                 ALU.mult, ALU.add))
        nc.vector.wait_ge(d_xre[t % 2].h, xre_loaded_box[t])
        if t >= 2:
            nc.vector.wait_ge(d_out[t % 2].h, 16 * (t // 2))
        nc.vector.tensor_tensor(e1[:, :], e1[:, :], e2[:, :], ALU.add)
        c_pool.inc(nc.vector.tensor_tensor(xre[t % 2][:, :], e1[:, :], xre[t % 2][:, :],
                                           ALU.add))
    for s in range(NS):
        for j in range(4):
            emit_e(s, j)
        if s + 2 < NS:
            emit_xrk(s + 2, wait_r=4 * (s + 1))

    nc.sync.wait_ge(c_pce.h, 53)
    xre_loaded = []
    for t in range(2):
        d_xre[t % 2].inc(nc.sync.dma_start(xre[t % 2][:, :],
                                           x_d[128 * t:128 * (t + 1), :]), 16)
        xre_loaded.append(d_xre[t % 2].n)
    for t in range(NT):
        nc.sync.wait_ge(c_pool.h, t + 1)
        d_out[t % 2].inc(nc.sync.dma_start(out_d[128 * t:128 * (t + 1), :],
                                           xre[t % 2][:, :]), 16)
        if t + 2 < NT:
            d_xre[t % 2].inc(nc.sync.dma_start(xre[t % 2][:, :],
                                               x_d[128 * (t + 2):128 * (t + 3), :]), 16)
            xre_loaded.append(d_xre[t % 2].n)

    nc.sync.wait_ge(d_out[0].h, 16 * (NT // 2))
    nc.sync.wait_ge(d_out[1].h, 16 * (NT // 2))

    psC.close()
    psVA.close()
    st2.close()
    st.close()
    return nc


def _dbg_finish(nc, g, stage, L):
    import concourse.mybir as mybir
    F32 = mybir.dt.float32
    d_dbg = L["st"].enter_context(nc.semaphore("d_dbg"))
    n = 0
    def dump(name, ap):
        nonlocal n
        o = nc.dram_tensor("dbg_" + name, list(ap.shape), F32, kind="ExternalOutput").ap()
        g.dma_start(o, ap).then_inc(d_dbg, 16)
        n += 16
    if stage >= 1:
        g.wait_ge(L["c_xnf"].h, 16)
        dump("xnf", L["xnf"][:, :])
    if stage >= 2:
        g.wait_ge(L["c_pre"].h, 1)
        dump("Pp", L["Pp"][:, :])
        dump("Pg", L["Pg"][:, :])
        dump("Pgt", L["Pgt"][0:64, :])
        dump("c0b", L["c0b"][:, :])
    if stage >= 3:
        g.wait_ge(L["c_proje"].h, 16)
        dump("SA", L["SA"][:, :])
        dump("SB", L["SB"][:, :])
        dump("SC", L["SC"][:, :])
        dump("SD", L["SD"][0:64, :])
    if stage >= 4:
        g.wait_ge(L["c_tri"].h, 1)
        dump("G", L["SB"][:, :])
        dump("TR", L["SD"][0:64, :])
    g.wait_ge(d_dbg, n)


_NC_CACHE = None


def kernel(**inputs):
    global _NC_CACHE
    f32 = lambda a: np.ascontiguousarray(np.asarray(a, dtype=np.float32))
    x = f32(inputs["L0_attn_x"])
    chunkT = lambda v: np.ascontiguousarray(f32(v).reshape(KC, 128).T)

    shared = dict(
        wq=f32(inputs["L0_attn_WQ"]),
        wk=f32(inputs["L0_attn_WK"]),
        winv=f32(inputs["L0_attn_Winv"]),
        woT=np.ascontiguousarray(f32(inputs["L0_attn_WO"]).T),
        wrT=np.ascontiguousarray(f32(inputs["L0_ffn_W_r"]).T),
        wkT=np.ascontiguousarray(f32(inputs["L0_ffn_W_k"]).T),
        wvT=np.ascontiguousarray(f32(inputs["L0_ffn_W_v"]).T),
        vb=f32(inputs["L0_attn_V_b"]),
        wb=f32(inputs["L0_attn_W_b"]),
        vt=f32(inputs["L0_attn_V_t"]),
        wt=f32(inputs["L0_attn_W_t"]),
        xt=f32(inputs["L0_attn_X_t"]),
        ub=f32(inputs["L0_attn_U_b"]),
        ut=f32(inputs["L0_attn_U_t"]),
        biasv=np.ascontiguousarray(
            np.concatenate([chunkT(inputs["L0_attn_bias_b"]),
                            chunkT(inputs["L0_attn_bias_t"])], axis=1)),
        mus=np.ascontiguousarray(
            np.concatenate([chunkT(inputs["L0_ffn_mu_r"]),
                            chunkT(inputs["L0_ffn_mu_k"])], axis=1)),
        ab=np.ascontiguousarray(np.broadcast_to(
            np.stack([np.float32(inputs["L0_attn_alpha_bi"]),
                      np.float32(inputs["L0_attn_alpha_tri"])]).astype(np.float32),
            (128, 2))),
        ident=np.eye(128, dtype=BFD),
        icbc=np.ascontiguousarray(np.broadcast_to(
            1.0 / np.arange(1, T + 1, dtype=np.float32), (64, T))),
        ictile=np.ascontiguousarray(
            (1.0 / (1.0 + np.arange(T, dtype=np.float32).reshape(NT, 128).T))),
    )

    if _NC_CACHE is None:
        _NC_CACHE = build_nc()
    in_maps = [dict(shared, x=np.ascontiguousarray(x[b])) for b in range(8)]
    res = run_bass_kernel_spmd(_NC_CACHE, in_maps, core_ids=list(range(8)))
    out = np.stack([np.asarray(res.results[b]["out"], dtype=np.float32)
                    for b in range(8)])
    return out


# version 30
# speedup vs baseline: 1.0177x; 1.0177x over previous
import numpy as np
import ml_dtypes
import concourse.bass as bass
import concourse.mybir as mybir
from concourse.bass_utils import run_bass_kernel_spmd
from contextlib import ExitStack

BFD = ml_dtypes.bfloat16
F32 = mybir.dt.float32
BF16 = mybir.dt.bfloat16
ALU = mybir.AluOpType
AF = mybir.ActivationFunctionType

T = 2048
D = 1024
R = 64
NT = 16
NS = 4
KC = 8
CW = 2052


class Sem:
    def __init__(self, nc, st, name):
        self.h = st.enter_context(nc.semaphore(name))
        self.n = 0

    def inc(self, inst, k=1):
        inst.then_inc(self.h, k)
        self.n += k
        return self.n


def build_nc(stage=5):
    nc = bass.Bass("TRN2", target_bir_lowering=False, detect_race_conditions=False)
    dt = nc.dram_tensor
    x_d = dt("x", [T, D], F32, kind="ExternalInput").ap()
    wq_d = dt("wq", [D, D], F32, kind="ExternalInput").ap()
    wk_d = dt("wk", [D, D], F32, kind="ExternalInput").ap()
    winv_d = dt("winv", [D, D], F32, kind="ExternalInput").ap()
    woT_d = dt("woT", [D, D], F32, kind="ExternalInput").ap()
    wrT_d = dt("wrT", [D, D], F32, kind="ExternalInput").ap()
    wkT_d = dt("wkT", [D, D], F32, kind="ExternalInput").ap()
    wvT_d = dt("wvT", [D, D], F32, kind="ExternalInput").ap()
    vb_d = dt("vb", [D, R], F32, kind="ExternalInput").ap()
    wb_d = dt("wb", [D, R], F32, kind="ExternalInput").ap()
    vt_d = dt("vt", [D, R], F32, kind="ExternalInput").ap()
    wt_d = dt("wt", [D, R], F32, kind="ExternalInput").ap()
    xt_d = dt("xt", [D, R], F32, kind="ExternalInput").ap()
    ub_d = dt("ub", [D, R], F32, kind="ExternalInput").ap()
    ut_d = dt("ut", [D, R], F32, kind="ExternalInput").ap()
    biasv_d = dt("biasv", [128, 16], F32, kind="ExternalInput").ap()
    mus_d = dt("mus", [128, 16], F32, kind="ExternalInput").ap()
    ab_d = dt("ab", [128, 2], F32, kind="ExternalInput").ap()
    ident_d = dt("ident", [128, 128], BF16, kind="ExternalInput").ap()
    icbc_d = dt("icbc", [64, T], F32, kind="ExternalInput").ap()
    ictile_d = dt("ictile", [128, NT], F32, kind="ExternalInput").ap()
    out_d = dt("out", [T, D], F32, kind="ExternalOutput").ap()

    st = ExitStack()
    sb = lambda n, s, d_: st.enter_context(nc.sbuf_tensor(n, s, d_))
    xnf = sb("xnf", [128, KC * CW], BF16)
    SA = sb("SA", [128, T], BF16)
    SB = sb("SB", [128, T], BF16)
    SC = sb("SC", [128, T], BF16)
    SD = sb("SD", [128, T], BF16)
    Pp = sb("Pp", [128, KC * 448], BF16)
    Pg = sb("Pg", [128, D], BF16)
    Pgt = sb("Pgt", [128, D], BF16)
    c0b = sb("c0b", [128, D], F32)
    c0row = sb("c0row", [1, D], BF16)
    icbc = sb("icbc_s", [128, T], F32)
    id_sb = sb("id_sb", [128, 128], BF16)
    ones1 = sb("ones1", [128, 1], BF16)
    onesr = sb("onesr", [1, 128], BF16)
    ictile = sb("ictile_s", [128, NT], F32)
    mus = sb("mus_s", [128, 16], F32)
    biasv = sb("biasv_s", [128, 16], F32)
    ab = sb("ab_s", [128, 2], F32)
    bcq = sb("bcq", [128, 8], BF16)
    bc32 = sb("bc32", [128, 8], F32)
    bct = sb("bct", [128, 8], F32)
    wrT = sb("wrT_s", [128, KC * D], BF16)
    wkT = sb("wkT_s", [128, KC * D], BF16)
    wvT = sb("wvT_s", [128, KC * D], BF16)
    xin = [sb(f"xin{i}", [128, D], F32) for i in range(4)]
    xntm = [sb(f"xntm{i}", [128, D], BF16) for i in range(2)]
    s1 = sb("s1", [128, NT], F32)
    s2 = sb("s2", [128, NT], F32)
    mean = sb("mean", [128, NT], F32)
    m2 = sb("m2", [128, NT], F32)
    var = sb("var", [128, NT], F32)
    rstdv = sb("rstdv", [128, NT], F32)
    eps1 = sb("eps1", [128, 1], F32)
    invd = sb("invd", [128, 1], F32)
    nhalf = sb("nhalf", [128, 1], F32)
    lnv = sb("lnv", [128, NT], F32)

    xnf_r = xnf[:, :].rearrange("p (k c) -> p k c", c=CW)
    Pp_r = Pp[:, :].rearrange("p (k c) -> p k c", c=448)

    sem = lambda n: Sem(nc, st, n)
    d_cst = sem("d_cst")
    d_sml = sem("d_sml")
    d_vb = sem("d_vb")
    d_x = [sem("d_x0"), sem("d_x1")]
    d_pw = sem("d_pw")
    d_wv1 = sem("d_wv1")
    d_fw1 = sem("d_fw1")
    d_fw2 = sem("d_fw2")
    d_fw3 = sem("d_fw3")
    d_a7 = sem("d_a7")
    d_icb = sem("d_icb")
    d_xre = [sem("d_xre0"), sem("d_xre1")]
    d_out = [sem("d_out0"), sem("d_out1")]
    c_sq = sem("c_sq")
    c_var = sem("c_var")
    c_sd = sem("c_sd")
    c_xn = sem("c_xn")
    c_tp = sem("c_tp")
    c_xnf = sem("c_xnf")
    c_pc = sem("c_pc")
    c_pce = sem("c_pce")
    c_bcq = sem("c_bcq")
    c_pre = sem("c_pre")
    c_proj = sem("c_proj")
    c_proje = sem("c_proje")
    c_scan4 = sem("c_scan4")
    c_tri = sem("c_tri")
    c_xrk = sem("c_xrk")
    c_k = sem("c_k")
    c_kr = sem("c_kr")
    c_ksq = sem("c_ksq")
    c_r = sem("c_r")
    c_sig = sem("c_sig")
    c_v = sem("c_v")
    c_at = sem("c_at")
    c_e12 = sem("c_e12")
    c_pool = sem("c_pool")
    c_bc0 = sem("c_bc0")

    c_ms = sem("c_ms")
    nc.gpsimd.memset(ones1[:, :], 1.0)
    nc.gpsimd.memset(onesr[:, :], 1.0)
    for _k in range(KC):
        nc.gpsimd.memset(xnf[:, CW * _k:CW * _k + 2], 0.0)
        nc.gpsimd.memset(xnf[:, CW * _k + 2050:CW * (_k + 1)], 0.0)
    nc.gpsimd.memset(eps1[:, :], 1e-5)
    nc.gpsimd.memset(invd[:, :], 1.0 / D)
    c_ms.inc(nc.gpsimd.memset(nhalf[:, :], -1.0))
    g = nc.gpsimd

    x_tiles_loaded = []
    def _xdma(t):
        if t >= 4:
            nc.sync.wait_ge(c_xn.h, t - 3)
        d_x[t % 2].inc(nc.sync.dma_start(xin[t % 4][:, :],
                                         x_d[128 * t:128 * (t + 1), :]), 16)
        x_tiles_loaded.append(d_x[t % 2].n)
    _xdma(0)
    _xdma(1)
    d_cst.inc(nc.sync.dma_start(id_sb[:, :], ident_d), 16)
    d_cst.inc(nc.sync.dma_start(ictile[:, :], ictile_d), 16)
    d_cst.inc(nc.sync.dma_start(mus[:, :], mus_d), 16)
    d_cst.inc(nc.sync.dma_start(biasv[:, :], biasv_d), 16)
    d_cst.inc(nc.sync.dma_start(ab[:, :], ab_d), 16)
    for t in range(2, NT):
        _xdma(t)
    d_icb.inc(nc.sync.dma_start(icbc[0:64, :], icbc_d), 16)

    st1o = ExitStack()
    sb1 = lambda n, s, d_: st1o.enter_context(nc.sbuf_tensor(n, s, d_))
    PAr = sb1("PAr", [128, KC * 192], BF16)
    PBr = sb1("PBr", [128, KC * 192], BF16)
    sqd = sb1("sqd", [128, D], BF16)
    nc.gpsimd.memset(sqd[:, :], 0.0)
    xcb = sb1("xcb", [128, D], F32)
    UU = sb1("UU", [128, KC * 128], BF16)
    uts = sb1("uts", [128, KC * 64], BF16)
    st1b = ExitStack()
    wq_s = st1b.enter_context(nc.sbuf_tensor("wq_s", [128, KC * D], BF16))
    wk_s = st1b.enter_context(nc.sbuf_tensor("wk_s", [128, KC * D], BF16))
    st1a = ExitStack()
    winv_s = st1a.enter_context(nc.sbuf_tensor("winv_s", [128, KC * D], BF16))
    vbwb = st1a.enter_context(nc.sbuf_tensor("vbwb", [128, KC * 128], BF16))
    PAr_r = PAr[:, :].rearrange("p (k c) -> p k c", c=192)
    PBr_r = PBr[:, :].rearrange("p (k c) -> p k c", c=192)
    UU_r = UU[:, :].rearrange("p (k c) -> p k c", c=128)
    uts_r = uts[:, :].rearrange("p (k c) -> p k c", c=64)
    vbwb_r = vbwb[:, :].rearrange("p (k c) -> p k c", c=128)

    def chunked(dr):
        return dr.rearrange("(k p) m -> p k m", p=128)

    d_vb.inc(g.dma_start(vbwb_r[:, :, 0:64], chunked(vb_d)), 16)
    d_vb.inc(g.dma_start(vbwb_r[:, :, 64:128], chunked(wb_d)), 16)
    winv_r0 = winv_s[:, :].rearrange("p (k c) -> p k c", c=D)
    d_wv1.inc(g.dma_start(winv_r0[:, :, 0:512], chunked(winv_d)[:, :, 0:512]), 16)
    d_pw.inc(g.dma_start(winv_r0[:, :, 512:D], chunked(winv_d)[:, :, 512:D]), 16)
    d_sml.inc(g.dma_start(PAr_r[:, :, 0:64], chunked(vb_d)), 16)
    d_sml.inc(g.dma_start(PAr_r[:, :, 128:192], chunked(vt_d)), 16)
    d_sml.inc(g.dma_start(PBr_r[:, :, 0:64], chunked(wb_d)), 16)
    d_sml.inc(g.dma_start(PBr_r[:, :, 128:192], chunked(wt_d)), 16)
    d_pw.inc(g.dma_start(wq_s[:, :].rearrange("p (k c) -> p k c", c=D),
                         chunked(wq_d)), 16)
    d_pw.inc(g.dma_start(wk_s[:, :].rearrange("p (k c) -> p k c", c=D),
                         chunked(wk_d)), 16)
    d_sml.inc(g.dma_start(Pp_r[:, :, 320:384], chunked(xt_d)), 16)
    d_sml.inc(g.dma_start(UU_r[:, :, 0:64], chunked(ub_d)), 16)
    d_sml.inc(g.dma_start(UU_r[:, :, 64:128], chunked(ub_d)), 16)
    d_sml.inc(g.dma_start(uts_r[:, :, :], chunked(ut_d)), 16)

    psA = ExitStack()
    psTP = [psA.enter_context(nc.psum_tensor(f"psTP{i}", [128, 1024], BF16)) for i in range(3)]
    psPC = [psA.enter_context(nc.psum_tensor(f"psPC{i}", [128, 512], F32)) for i in range(2)]
    psPG = psA.enter_context(nc.psum_tensor("psPG", [128, 1024], F32))

    nc.vector.wait_ge(d_cst.h, d_cst.n)
    nc.vector.scalar_tensor_tensor(bc32[:, :], biasv[:, 0:8], ab[:, 0:1], biasv[:, 0:8],
                                   ALU.mult, ALU.add)
    c_bc0.inc(nc.vector.tensor_scalar(bct[:, :], biasv[:, 8:16], ab[:, 1:2], None, ALU.mult))

    def emit_dve_ln_iter(it, extra=None):
        if it < NT:
            nc.vector.wait_ge(d_x[it % 2].h, x_tiles_loaded[it])
            nc.vector.reduce_sum(s1[:, it:it + 1], xin[it % 4][:, :],
                                 axis=mybir.AxisListType.X)
        elif it == NT:
            nc.vector.tensor_copy(sqd[:, 0:256], sqd[:, 0:256])
        if 1 <= it <= NT:
            t = it - 1
            nc.vector.wait_ge(c_sd.h, t + 1)
            nc.vector.reciprocal(rstdv[:, t:t + 1], lnv[:, t:t + 1])
        if 2 <= it <= NT + 1:
            t = it - 2
            nc.vector.wait_ge(c_tp.h, t + 1)
            srcv = psTP[t % 3][:, :].rearrange("p (k c) -> p k c", c=128)
            nc.vector.tensor_copy(
                xnf_r[:, :, 2 + 128 * t:2 + 128 * t + 64],
                srcv[:, :, 0:64])
        else:
            nc.vector.tensor_copy(sqd[:, 0:256], sqd[:, 0:256])
        if 1 <= it <= NT:
            t = it - 1
            c_xn.inc(nc.vector.tensor_scalar(xntm[t % 2][:, :], xin[t % 4][:, :],
                                             mean[:, t:t + 1], rstdv[:, t:t + 1],
                                             ALU.subtract, ALU.mult))
        else:
            nc.vector.tensor_copy(sqd[:, 0:256], sqd[:, 0:256])
        if it < NT:
            nc.vector.tensor_scalar(mean[:, it:it + 1], s1[:, it:it + 1],
                                    invd[:, 0:1], None, ALU.mult)
        if 2 <= it <= NT + 1:
            t = it - 2
            srcv = psTP[t % 3][:, :].rearrange("p (k c) -> p k c", c=128)
            c_xnf.inc(nc.vector.tensor_copy(
                xnf_r[:, :, 2 + 128 * t + 64:2 + 128 * (t + 1)],
                srcv[:, :, 64:128]))
        else:
            nc.vector.tensor_copy(sqd[:, 0:256], sqd[:, 0:256])
        if it < NT:
            c_var.inc(nc.vector.tensor_scalar(m2[:, it:it + 1], mean[:, it:it + 1],
                                              mean[:, it:it + 1], nhalf[:, 0:1],
                                              ALU.mult, ALU.mult))
        if extra is not None:
            extra()

    def emit_act_ln_iter(it, extra=None):
        if 1 <= it <= NT:
            t = it - 1
            nc.scalar.wait_ge(c_var.h, t + 1)
            nc.scalar.activation(var[:, t:t + 1], s2[:, t:t + 1], AF.Identity,
                                 scale=invd[:, 0:1], bias=m2[:, t:t + 1])
            c_sd.inc(nc.scalar.activation(lnv[:, t:t + 1], var[:, t:t + 1], AF.Sqrt,
                                          bias=eps1[:, 0:1]))
        if it < NT:
            nc.scalar.wait_ge(d_x[it % 2].h, x_tiles_loaded[it])
            if it == 0:
                nc.scalar.wait_ge(c_ms.h, 1)
            nc.scalar.activation(sqd[:, :], xin[it % 4][:, :], AF.Square,
                                 accum_out=s2[:, it:it + 1])
        if extra is not None:
            extra()

    def emit_pe_tp(t):
        if t == 0:
            nc.tensor.wait_ge(d_cst.h, d_cst.n)
            nc.tensor.wait_ge(c_ms.h, 1)
        nc.tensor.wait_ge(c_xn.h, t + 1)
        if t >= 3:
            nc.tensor.wait_ge(c_xnf.h, t - 2)
        for k in range(KC):
            inst = nc.tensor.transpose(psTP[t % 3][:, 128 * k:128 * (k + 1)],
                                       xntm[t % 2][:, 128 * k:128 * (k + 1)], id_sb[:, :])
            if k == KC - 1:
                c_tp.inc(inst)

    winv_r = winv_s[:, :].rearrange("p (k c) -> p k c", c=D)
    wqr_box = {}

    def emit_tmp12(m):
        if m == 0:
            nc.tensor.wait_ge(d_wv1.h, 16)
            nc.tensor.wait_ge(d_vb.h, 32)
        if m == 4:
            nc.tensor.wait_ge(d_pw.h, 16)
        if m >= 2:
            nc.tensor.wait_ge(c_pce.h, 2 * (m - 1))
        for k in range(KC):
            inst = nc.tensor.matmul(psPC[m % 2][:, 0:128],
                                    winv_r[:, k, 128 * m:128 * (m + 1)],
                                    vbwb[:, 128 * k:128 * (k + 1)],
                                    start=(k == 0), stop=(k == KC - 1))
        c_pc.inc(inst)

    def emit_pa(m):
        if m == 0:
            nc.tensor.wait_ge(d_pw.h, 32)
            nc.tensor.wait_ge(d_sml.h, 64)
            nc.tensor.wait_ge(c_pce.h, 16)
        if m >= 2:
            nc.tensor.wait_ge(c_pce.h, 16 + 2 * (m - 1))
        for k in range(KC):
            inst = nc.tensor.matmul(psPC[m % 2][:, 0:192],
                                    wqr_box["wq"][:, k, 128 * m:128 * (m + 1)],
                                    PAr[:, 192 * k:192 * (k + 1)],
                                    start=(k == 0), stop=(k == KC - 1))
        c_pc.inc(inst)

    def emit_pb(m):
        if m == 0:
            nc.tensor.wait_ge(d_pw.h, 48)
        nc.tensor.wait_ge(c_pce.h, 30 + 2 * m)
        for k in range(KC):
            inst = nc.tensor.matmul(psPC[m % 2][:, 0:192],
                                    wqr_box["wk"][:, k, 128 * m:128 * (m + 1)],
                                    PBr[:, 192 * k:192 * (k + 1)],
                                    start=(k == 0), stop=(k == KC - 1))
        c_pc.inc(inst)

    pe_slots = {2: [0, 1], 3: [2, 3], 4: [4, 5], 5: [6, 7],
                11: [8, 9], 12: [10, 11], 13: [12, 13], 14: [14, 15],
                15: [16, 17]}
    pre_units = ([lambda m=m: emit_tmp12(m) for m in range(KC)]
                 + [lambda m=m: emit_pa(m) for m in range(KC)]
                 + [lambda m=m: emit_pb(m) for m in range(KC)])

    for t in range(NT):
        emit_pe_tp(t)
        if stage >= 2:
            for u in pe_slots.get(t, []):
                pre_units[u]()
            if t == 8:
                st1a.close()
                woT_s = st1b.enter_context(nc.sbuf_tensor("woT_s", [128, KC * D], BF16))
                g.wait_ge(c_pc.h, 8)
                d_pw.inc(g.dma_start(woT_s[:, :].rearrange("p (k c) -> p k c", c=D),
                                     woT_d.rearrange("(k p) m -> p k m", p=128)), 16)
                g.wait_ge(d_x[0].h, 16 * (NT // 2))
                g.wait_ge(d_x[1].h, 16 * (NT // 2))
                d_fw1.inc(g.dma_start(wkT[:, :].rearrange("p (k c) -> p k c", c=D),
                                      wkT_d.rearrange("(k p) m -> p k m", p=128)), 16)
                d_fw2.inc(g.dma_start(wrT[:, :].rearrange("p (k c) -> p k c", c=D),
                                      wrT_d.rearrange("(k p) m -> p k m", p=128)), 16)
                d_fw3.inc(g.dma_start(wvT[:, :].rearrange("p (k c) -> p k c", c=D),
                                      wvT_d.rearrange("(k p) m -> p k m", p=128)), 16)
                wqr_box["wq"] = wq_s[:, :].rearrange("p (k c) -> p k c", c=D)
                wqr_box["wk"] = wk_s[:, :].rearrange("p (k c) -> p k c", c=D)
                wqr_box["woT"] = woT_s[:, :].rearrange("p (k c) -> p k c", c=D)

    if stage < 2:
        for it in range(NT + 2):
            emit_dve_ln_iter(it)
        _dbg_finish(nc, g, stage, locals())
        st1a.close(); st1b.close(); st1o.close(); psA.close(); st.close()
        return nc
    for u in range(18, 24):
        pre_units[u]()

    woT_r = wqr_box["woT"]
    nc.tensor.wait_ge(d_pw.h, 64)
    nc.tensor.wait_ge(d_sml.h, 128)
    nc.tensor.wait_ge(c_bcq.h, 1)
    for h in range(2):
        for k in range(KC):
            inst = nc.tensor.matmul(psPG[0:1, 512 * h:512 * (h + 1)],
                                    bcq[:, k:k + 1], woT_r[:, k, 512 * h:512 * (h + 1)],
                                    start=(k == 0), stop=(k == KC - 1))
    c_pc.inc(inst)
    nc.tensor.wait_ge(c_pce.h, 49)
    for h in range(2):
        for k in range(KC):
            inst = nc.tensor.matmul(psPG[:, 512 * h:512 * (h + 1)],
                                    UU[:, 128 * k:128 * (k + 1)],
                                    woT_r[:, k, 512 * h:512 * (h + 1)],
                                    start=(k == 0), stop=(k == KC - 1))
    c_pc.inc(inst)
    for h in range(2):
        inst = nc.tensor.matmul(psPC[h][:, :], onesr[:, :],
                                c0row[:, 512 * h:512 * (h + 1)], start=True, stop=True)
    c_pc.inc(inst)
    nc.tensor.wait_ge(c_pce.h, 50)
    for h in range(2):
        for k in range(KC):
            inst = nc.tensor.matmul(psPG[0:64, 512 * h:512 * (h + 1)],
                                    uts[:, 64 * k:64 * (k + 1)],
                                    woT_r[:, k, 512 * h:512 * (h + 1)],
                                    start=(k == 0), stop=(k == KC - 1))
    c_pc.inc(inst)

    def act_tmp12_evac(m):
        nc.scalar.wait_ge(c_pc.h, m + 1)
        c_pce.inc(nc.scalar.activation(PAr_r[:, m, 64:128], psPC[m % 2][:, 64:128],
                                       AF.Identity, scale=ab[:, 0:1]))
        c_pce.inc(nc.scalar.copy(PBr_r[:, m, 64:128], psPC[m % 2][:, 0:64]))

    def act_pa_evac(m):
        nc.scalar.wait_ge(c_pc.h, 9 + m)
        c_pce.inc(nc.scalar.copy(Pp_r[:, m, 0:128], psPC[m % 2][:, 0:128]))
        c_pce.inc(nc.scalar.copy(Pp_r[:, m, 256:320], psPC[m % 2][:, 128:192]))

    def act_pb_evac(m):
        nc.scalar.wait_ge(c_pc.h, 17 + m)
        c_pce.inc(nc.scalar.copy(Pp_r[:, m, 128:256], psPC[m % 2][:, 0:128]))
        c_pce.inc(nc.scalar.copy(Pp_r[:, m, 384:448], psPC[m % 2][:, 128:192]))

    act_slots = {4: [0, 1], 5: [2, 3], 6: [4, 5], 7: [6, 7],
                 14: [8, 9], 15: [10, 11], 16: [12, 13], 17: [14, 15]}
    act_units = ([lambda m=m: act_tmp12_evac(m) for m in range(KC)]
                 + [lambda m=m: act_pa_evac(m) for m in range(KC)]
                 + [lambda m=m: act_pb_evac(m) for m in range(KC)])

    for it in range(NT + 2):
        def _extra(it=it):
            for u in act_slots.get(it, []):
                act_units[u]()
        emit_act_ln_iter(it, _extra)
    for u in range(16, 24):
        act_units[u]()
    nc.scalar.wait_ge(c_pc.h, 25)
    c_pce.inc(nc.scalar.copy(c0row[:, :], psPG[0:1, :]))
    nc.scalar.wait_ge(c_pc.h, 26)
    c_pce.inc(nc.scalar.copy(Pg[:, :], psPG[:, :]))
    nc.scalar.wait_ge(c_pc.h, 27)
    c_pce.inc(nc.scalar.copy(c0b[:, 0:512], psPC[0][:, :]))
    c_pce.inc(nc.scalar.copy(c0b[:, 512:1024], psPC[1][:, :]))
    nc.scalar.wait_ge(c_pc.h, 28)
    c_pce.inc(nc.scalar.activation(Pgt[0:64, :], psPG[0:64, :], AF.Identity,
                                   scale=ab[0:64, 1:2]))

    def _dve_extra(it):
        if it == 2:
            nc.vector.wait_ge(c_bc0.h, 1)
            nc.vector.tensor_tensor(bc32[:, :], bc32[:, :], bct[:, :], ALU.add)
        elif it == 4:
            c_bcq.inc(nc.vector.tensor_copy(bcq[:, :], bc32[:, :]))
    for it in range(NT + 2):
        emit_dve_ln_iter(it, lambda it=it: _dve_extra(it))

    st1b.close()
    st1o.close()
    psA.close()

    st2 = ExitStack()
    sb2 = lambda n, s, d_: st2.enter_context(nc.sbuf_tensor(n, s, d_))
    xrb = [sb2(f"xrb{i}", [128, KC * 512], BF16) for i in range(2)]
    xkb = [sb2(f"xkb{i}", [128, KC * 512], BF16) for i in range(2)]
    kact = sb2("kact", [128, KC * 512], BF16)
    dbuf = sb2("dbuf", [128, 512], BF16)
    kr = [sb2(f"kr{i}", [128, 512], BF16) for i in range(4)]
    r_tm = [sb2(f"r_tm{i}", [128, D], BF16) for i in range(2)]
    e1 = sb2("e1", [128, D], F32)
    e2 = sb2("e2", [128, D], F32)
    xre = [sb2(f"xre{i}", [128, D], F32) for i in range(2)]

    if stage < 3:
        _dbg_finish(nc, g, stage, locals())
        st2.close(); st.close()
        return nc
    psVA = ExitStack()
    psV = psVA.enter_context(nc.psum_tensor("psV", [128, 1024], F32))
    psAT = psVA.enter_context(nc.psum_tensor("psAT", [128, 1024], F32))
    psB = ExitStack()
    pj = [psB.enter_context(nc.psum_tensor(f"pj{p}", [128, 512], F32))
          for p in range(4)]

    nc.tensor.wait_ge(c_xnf.h, NT)
    nc.tensor.wait_ge(c_pce.h, 53)
    proj_groups = []
    for s in range(NS):
        for p in range(4):
            if s >= 1:
                nc.tensor.wait_ge(c_proje.h, 4 * (s - 1) + p + 1)
            dst = pj[p][0:64, :] if p == 3 else pj[p][:, :]
            for k in range(KC):
                lcol = 448 * k + (384 if p == 3 else 128 * p)
                lw = 64 if p == 3 else 128
                inst = nc.tensor.matmul(dst, Pp[:, lcol:lcol + lw],
                                        xnf_r[:, k, 2 + 512 * s:2 + 512 * (s + 1)],
                                        start=(k == 0), stop=(k == KC - 1))
            c_proj.inc(inst)
            proj_groups.append((s, p, c_proj.n))

    evac_count = {}
    _ec = 0
    for s, p, pn in proj_groups:
        nc.scalar.wait_ge(c_proj.h, pn)
        if p == 0 or p == 1:
            dst = SA if p == 0 else SB
            c_proje.inc(nc.scalar.copy(dst[:, 512 * s:512 * (s + 1)],
                                       pj[p][:, :]))
            _ec += 1
        elif p == 2:
            nc.scalar.copy(SC[0:64, 512 * s:512 * (s + 1)], pj[2][0:64, :])
            c_proje.inc(nc.scalar.copy(SD[64:128, 512 * s:512 * (s + 1)],
                                       pj[2][64:128, :]))
            c_proje.n += 0
            _ec += 1
            c_proje.inc(nc.scalar.engine_nop()) if False else None
            _ec += 0
        else:
            c_proje.inc(nc.scalar.copy(SD[0:64, 512 * s:512 * (s + 1)],
                                       pj[3][0:64, :]))
            _ec += 1
        evac_count[(s, p)] = c_proje.n

    if stage < 4:
        _dbg_finish(nc, g, stage, locals())
        psB.close(); psVA.close(); st2.close(); st.close()
        return nc
    psB.close()
    psC = ExitStack()
    psK = [psC.enter_context(nc.psum_tensor(f"psK{i}", [128, 512], F32)) for i in range(2)]
    psR = psC.enter_context(nc.psum_tensor("psR", [128, 1024], F32))

    def emit_xrk(s, wait_r=None):
        nc.vector.wait_ge(c_pc.h, 28)
        if wait_r is not None:
            nc.vector.wait_ge(c_r.h, wait_r)
        for c in range(KC):
            xc = xnf_r[:, c, 2 + 512 * s:2 + 512 * (s + 1)]
            xp = xnf_r[:, c, 1 + 512 * s:1 + 512 * (s + 1)]
            nc.vector.tensor_tensor(dbuf[:, :], xc, xp, ALU.subtract)
            nc.vector.scalar_tensor_tensor(xrb[s % 2][:, 512 * c:512 * (c + 1)],
                                           dbuf[:, :], mus[:, c:c + 1], xp,
                                           ALU.mult, ALU.add)
            inst = nc.vector.scalar_tensor_tensor(xkb[s % 2][:, 512 * c:512 * (c + 1)],
                                                  dbuf[:, :], mus[:, 8 + c:9 + c], xp,
                                                  ALU.mult, ALU.add)
        c_xrk.inc(inst)

    emit_xrk(0)

    H = T // 2
    ones_h = ones1[:, 0:1].broadcast_to((128, H))
    nc.vector.wait_ge(c_ms.h, 1)
    nc.vector.wait_ge(c_proje.h, 4 * 1 + 2)
    nc.vector.tensor_tensor_scan(psV[:, :], ones_h, SB[:, 0:H], 0.0, ALU.mult, ALU.add)
    nc.vector.tensor_tensor(SB[:, 0:H], SA[:, 0:H], psV[:, :], ALU.mult)
    nc.vector.wait_ge(c_proje.h, 4 * 3 + 2)
    nc.vector.tensor_tensor_scan(psAT[:, :], ones_h, SB[:, H:T],
                                 psV[:, H - 1:H], ALU.mult, ALU.add)
    nc.vector.tensor_tensor(SB[:, H:T], SA[:, H:T], psAT[:, :], ALU.mult)
    nc.vector.wait_ge(c_proje.h, 4 * 1 + 4)
    nc.vector.tensor_tensor_scan(psV[:, :], ones_h, SD[:, 0:H], 0.0, ALU.mult, ALU.add)
    nc.vector.tensor_copy(SC[64:128, 0:H], psV[64:128, :])
    nc.vector.wait_ge(c_proje.h, 16)
    nc.vector.tensor_tensor_scan(psAT[:, :], ones_h, SD[:, H:T],
                                 psV[:, H - 1:H], ALU.mult, ALU.add)
    c_scan4.inc(nc.vector.tensor_copy(SC[64:128, H:T], psAT[64:128, :]))
    g.wait_ge(c_scan4.h, 1)
    d_a7.inc(g.dma_start(SA[0:64, :], SC[64:128, :]), 16)
    nc.vector.tensor_tensor(SD[0:64, 0:H], SC[0:64, 0:H], psV[0:64, :], ALU.mult)
    nc.vector.tensor_tensor(SD[0:64, H:T], SC[0:64, H:T], psAT[0:64, :], ALU.mult)
    nc.vector.wait_ge(d_a7.h, 16)
    nc.vector.wait_ge(d_icb.h, 16)
    nc.vector.tensor_tensor(SC[0:64, :], SA[0:64, :], icbc[0:64, :], ALU.mult)
    c_tri.inc(nc.vector.tensor_tensor(SD[0:64, :], SD[0:64, :], SC[0:64, :], ALU.mult))
    emit_xrk(1)

    if stage < 5:
        _dbg_finish(nc, g, stage, locals())
        psC.close(); psVA.close(); st2.close(); st.close()
        return nc
    nc.tensor.wait_ge(c_proje.h, 16)
    nc.tensor.wait_ge(d_fw1.h, 16)
    for s in range(NS):
        nc.tensor.wait_ge(c_xrk.h, s + 1)
        for d8 in range(KC):
            gi = KC * s + d8
            if gi >= 2:
                nc.tensor.wait_ge(c_kr.h, gi - 1)
            for c in range(KC):
                inst = nc.tensor.matmul(psK[d8 % 2][:, :],
                                        wkT[:, 1024 * c + 128 * d8:1024 * c + 128 * (d8 + 1)],
                                        xkb[s % 2][:, 512 * c:512 * (c + 1)],
                                        start=(c == 0), stop=(c == KC - 1))
            c_k.inc(inst)
        if s == 0:
            nc.tensor.wait_ge(d_fw2.h, 16)
            nc.tensor.wait_ge(d_fw3.h, 16)
        for j in range(4):
            t = 4 * s + j
            if t >= 1:
                nc.tensor.wait_ge(c_sig.h, t)
            for c in range(KC):
                lhsT = xrb[s % 2][:, 512 * c + 128 * j:512 * c + 128 * (j + 1)]
                nc.tensor.matmul(psR[:, 0:512], lhsT, wrT[:, 1024 * c:1024 * c + 512],
                                 start=(c == 0), stop=(c == KC - 1))
                inst = nc.tensor.matmul(psR[:, 512:1024], lhsT,
                                        wrT[:, 1024 * c + 512:1024 * (c + 1)],
                                        start=(c == 0), stop=(c == KC - 1))
            c_r.inc(inst)
            if s == 0 and j == 0:
                nc.tensor.wait_ge(c_tri.h, 1)
            if j == 0:
                nc.tensor.wait_ge(c_ksq.h, KC * (s + 1))
            if t >= 1:
                nc.tensor.wait_ge(c_e12.h, 2 * t)
            for d8 in range(KC):
                lhsT = kact[:, 512 * d8 + 128 * j:512 * d8 + 128 * (j + 1)]
                nc.tensor.matmul(psV[:, 0:512], lhsT, wvT[:, 1024 * d8:1024 * d8 + 512],
                                 start=(d8 == 0), stop=(d8 == KC - 1))
                inst = nc.tensor.matmul(psV[:, 512:1024], lhsT,
                                        wvT[:, 1024 * d8 + 512:1024 * (d8 + 1)],
                                        start=(d8 == 0), stop=(d8 == KC - 1))
            c_v.inc(inst)
            for h in range(2):
                nc.tensor.matmul(psAT[:, 512 * h:512 * (h + 1)],
                                 SB[:, 128 * t:128 * (t + 1)],
                                 Pg[:, 512 * h:512 * (h + 1)], start=True, stop=False)
                inst = nc.tensor.matmul(psAT[:, 512 * h:512 * (h + 1)],
                                        SD[0:64, 128 * t:128 * (t + 1)],
                                        Pgt[0:64, 512 * h:512 * (h + 1)],
                                        start=False, stop=True)
            c_at.inc(inst)

    for s in range(NS):
        for d8 in range(KC):
            gi = KC * s + d8
            nc.scalar.wait_ge(c_k.h, gi + 1)
            c_kr.inc(nc.scalar.activation(kr[gi % 4][:, :], psK[d8 % 2][:, :], AF.Relu))
            c_ksq.inc(nc.scalar.activation(kact[:, 512 * d8:512 * (d8 + 1)],
                                           kr[gi % 4][:, :], AF.Square))
        for j in range(4):
            t = 4 * s + j
            nc.scalar.wait_ge(c_r.h, t + 1)
            if t >= 2:
                nc.scalar.wait_ge(c_e12.h, 2 * t - 2)
            c_sig.inc(nc.scalar.activation(r_tm[j % 2][:, :], psR[:, :], AF.Sigmoid))

    xre_loaded_box = {t: 16 * (t // 2 + 1) for t in range(NT)}
    def emit_e(s, j):
        t = 4 * s + j
        nc.vector.wait_ge(c_v.h, t + 1)
        nc.vector.wait_ge(c_sig.h, t + 1)
        c_e12.inc(nc.vector.tensor_tensor(e2[:, :], r_tm[j % 2][:, :], psV[:, :],
                                          ALU.mult))
        nc.vector.wait_ge(c_at.h, t + 1)
        c_e12.inc(nc.vector.scalar_tensor_tensor(e1[:, :], psAT[:, :],
                                                 ictile[:, t:t + 1], c0b[:, :],
                                                 ALU.mult, ALU.add))
        nc.vector.wait_ge(d_xre[t % 2].h, xre_loaded_box[t])
        if t >= 2:
            nc.vector.wait_ge(d_out[t % 2].h, 16 * (t // 2))
        nc.vector.tensor_tensor(e1[:, :], e1[:, :], e2[:, :], ALU.add)
        c_pool.inc(nc.vector.tensor_tensor(xre[t % 2][:, :], e1[:, :], xre[t % 2][:, :],
                                           ALU.add))
    for s in range(NS):
        for j in range(4):
            emit_e(s, j)
        if s + 2 < NS:
            emit_xrk(s + 2, wait_r=4 * (s + 1))

    nc.sync.wait_ge(c_pce.h, 53)
    xre_loaded = []
    for t in range(2):
        d_xre[t % 2].inc(nc.sync.dma_start(xre[t % 2][:, :],
                                           x_d[128 * t:128 * (t + 1), :]), 16)
        xre_loaded.append(d_xre[t % 2].n)
    for t in range(NT):
        nc.sync.wait_ge(c_pool.h, t + 1)
        d_out[t % 2].inc(nc.sync.dma_start(out_d[128 * t:128 * (t + 1), :],
                                           xre[t % 2][:, :]), 16)
        if t + 2 < NT:
            d_xre[t % 2].inc(nc.sync.dma_start(xre[t % 2][:, :],
                                               x_d[128 * (t + 2):128 * (t + 3), :]), 16)
            xre_loaded.append(d_xre[t % 2].n)

    nc.sync.wait_ge(d_out[0].h, 16 * (NT // 2))
    nc.sync.wait_ge(d_out[1].h, 16 * (NT // 2))

    psC.close()
    psVA.close()
    st2.close()
    st.close()
    return nc


def _dbg_finish(nc, g, stage, L):
    import concourse.mybir as mybir
    F32 = mybir.dt.float32
    d_dbg = L["st"].enter_context(nc.semaphore("d_dbg"))
    n = 0
    def dump(name, ap):
        nonlocal n
        o = nc.dram_tensor("dbg_" + name, list(ap.shape), F32, kind="ExternalOutput").ap()
        g.dma_start(o, ap).then_inc(d_dbg, 16)
        n += 16
    if stage >= 1:
        g.wait_ge(L["c_xnf"].h, 16)
        dump("xnf", L["xnf"][:, :])
    if stage >= 2:
        g.wait_ge(L["c_pre"].h, 1)
        dump("Pp", L["Pp"][:, :])
        dump("Pg", L["Pg"][:, :])
        dump("Pgt", L["Pgt"][0:64, :])
        dump("c0b", L["c0b"][:, :])
    if stage >= 3:
        g.wait_ge(L["c_proje"].h, 16)
        dump("SA", L["SA"][:, :])
        dump("SB", L["SB"][:, :])
        dump("SC", L["SC"][:, :])
        dump("SD", L["SD"][0:64, :])
    if stage >= 4:
        g.wait_ge(L["c_tri"].h, 1)
        dump("G", L["SB"][:, :])
        dump("TR", L["SD"][0:64, :])
    g.wait_ge(d_dbg, n)


_NC_CACHE = None


def kernel(**inputs):
    global _NC_CACHE
    f32 = lambda a: np.ascontiguousarray(np.asarray(a, dtype=np.float32))
    x = f32(inputs["L0_attn_x"])
    chunkT = lambda v: np.ascontiguousarray(f32(v).reshape(KC, 128).T)

    shared = dict(
        wq=f32(inputs["L0_attn_WQ"]),
        wk=f32(inputs["L0_attn_WK"]),
        winv=f32(inputs["L0_attn_Winv"]),
        woT=np.ascontiguousarray(f32(inputs["L0_attn_WO"]).T),
        wrT=np.ascontiguousarray(f32(inputs["L0_ffn_W_r"]).T),
        wkT=np.ascontiguousarray(f32(inputs["L0_ffn_W_k"]).T),
        wvT=np.ascontiguousarray(f32(inputs["L0_ffn_W_v"]).T),
        vb=f32(inputs["L0_attn_V_b"]),
        wb=f32(inputs["L0_attn_W_b"]),
        vt=f32(inputs["L0_attn_V_t"]),
        wt=f32(inputs["L0_attn_W_t"]),
        xt=f32(inputs["L0_attn_X_t"]),
        ub=f32(inputs["L0_attn_U_b"]),
        ut=f32(inputs["L0_attn_U_t"]),
        biasv=np.ascontiguousarray(
            np.concatenate([chunkT(inputs["L0_attn_bias_b"]),
                            chunkT(inputs["L0_attn_bias_t"])], axis=1)),
        mus=np.ascontiguousarray(
            np.concatenate([chunkT(inputs["L0_ffn_mu_r"]),
                            chunkT(inputs["L0_ffn_mu_k"])], axis=1)),
        ab=np.ascontiguousarray(np.broadcast_to(
            np.stack([np.float32(inputs["L0_attn_alpha_bi"]),
                      np.float32(inputs["L0_attn_alpha_tri"])]).astype(np.float32),
            (128, 2))),
        ident=np.eye(128, dtype=BFD),
        icbc=np.ascontiguousarray(np.broadcast_to(
            1.0 / np.arange(1, T + 1, dtype=np.float32), (64, T))),
        ictile=np.ascontiguousarray(
            (1.0 / (1.0 + np.arange(T, dtype=np.float32).reshape(NT, 128).T))),
    )

    if _NC_CACHE is None:
        _NC_CACHE = build_nc()
    in_maps = [dict(shared, x=np.ascontiguousarray(x[b])) for b in range(8)]
    res = run_bass_kernel_spmd(_NC_CACHE, in_maps, core_ids=list(range(8)))
    out = np.stack([np.asarray(res.results[b]["out"], dtype=np.float32)
                    for b in range(8)])
    return out


# version 31
# speedup vs baseline: 1.0246x; 1.0068x over previous
import numpy as np
import ml_dtypes
import concourse.bass as bass
import concourse.mybir as mybir
from concourse.bass_utils import run_bass_kernel_spmd
from contextlib import ExitStack

BFD = ml_dtypes.bfloat16
F32 = mybir.dt.float32
BF16 = mybir.dt.bfloat16
ALU = mybir.AluOpType
AF = mybir.ActivationFunctionType

T = 2048
D = 1024
R = 64
NT = 16
NS = 4
KC = 8
CW = 2052


class Sem:
    def __init__(self, nc, st, name):
        self.h = st.enter_context(nc.semaphore(name))
        self.n = 0

    def inc(self, inst, k=1):
        inst.then_inc(self.h, k)
        self.n += k
        return self.n


def build_nc(stage=5):
    nc = bass.Bass("TRN2", target_bir_lowering=False, detect_race_conditions=False)
    dt = nc.dram_tensor
    x_d = dt("x", [T, D], F32, kind="ExternalInput").ap()
    wq_d = dt("wq", [D, D], F32, kind="ExternalInput").ap()
    wk_d = dt("wk", [D, D], F32, kind="ExternalInput").ap()
    winv_d = dt("winv", [D, D], F32, kind="ExternalInput").ap()
    woT_d = dt("woT", [D, D], F32, kind="ExternalInput").ap()
    wrT_d = dt("wrT", [D, D], F32, kind="ExternalInput").ap()
    wkT_d = dt("wkT", [D, D], F32, kind="ExternalInput").ap()
    wvT_d = dt("wvT", [D, D], F32, kind="ExternalInput").ap()
    vb_d = dt("vb", [D, R], F32, kind="ExternalInput").ap()
    wb_d = dt("wb", [D, R], F32, kind="ExternalInput").ap()
    vt_d = dt("vt", [D, R], F32, kind="ExternalInput").ap()
    wt_d = dt("wt", [D, R], F32, kind="ExternalInput").ap()
    xt_d = dt("xt", [D, R], F32, kind="ExternalInput").ap()
    ub_d = dt("ub", [D, R], F32, kind="ExternalInput").ap()
    ut_d = dt("ut", [D, R], F32, kind="ExternalInput").ap()
    biasv_d = dt("biasv", [128, 16], F32, kind="ExternalInput").ap()
    mus_d = dt("mus", [128, 16], F32, kind="ExternalInput").ap()
    ab_d = dt("ab", [128, 2], F32, kind="ExternalInput").ap()
    ident_d = dt("ident", [128, 128], BF16, kind="ExternalInput").ap()
    icbc_d = dt("icbc", [64, T], F32, kind="ExternalInput").ap()
    ictile_d = dt("ictile", [128, NT], F32, kind="ExternalInput").ap()
    out_d = dt("out", [T, D], F32, kind="ExternalOutput").ap()

    st = ExitStack()
    sb = lambda n, s, d_: st.enter_context(nc.sbuf_tensor(n, s, d_))
    xnf = sb("xnf", [128, KC * CW], BF16)
    SA = sb("SA", [128, T], BF16)
    SB = sb("SB", [128, T], BF16)
    SC = sb("SC", [128, T], BF16)
    SD = sb("SD", [128, T], BF16)
    Pp = sb("Pp", [128, KC * 448], BF16)
    Pg = sb("Pg", [128, D], BF16)
    Pgt = sb("Pgt", [128, D], BF16)
    c0b = sb("c0b", [128, D], F32)
    c0row = sb("c0row", [1, D], BF16)
    icbc = sb("icbc_s", [128, T], F32)
    id_sb = sb("id_sb", [128, 128], BF16)
    ones1 = sb("ones1", [128, 1], BF16)
    onesr = sb("onesr", [1, 128], BF16)
    ictile = sb("ictile_s", [128, NT], F32)
    mus = sb("mus_s", [128, 16], F32)
    biasv = sb("biasv_s", [128, 16], F32)
    ab = sb("ab_s", [128, 2], F32)
    bcq = sb("bcq", [128, 8], BF16)
    bc32 = sb("bc32", [128, 8], F32)
    bct = sb("bct", [128, 8], F32)
    wrT = sb("wrT_s", [128, KC * D], BF16)
    wkT = sb("wkT_s", [128, KC * D], BF16)
    wvT = sb("wvT_s", [128, KC * D], BF16)
    xin = [sb(f"xin{i}", [128, D], F32) for i in range(4)]
    xntm = [sb(f"xntm{i}", [128, D], BF16) for i in range(2)]
    s1 = sb("s1", [128, NT], F32)
    s2 = sb("s2", [128, NT], F32)
    mean = sb("mean", [128, NT], F32)
    m2 = sb("m2", [128, NT], F32)
    var = sb("var", [128, NT], F32)
    rstdv = sb("rstdv", [128, NT], F32)
    eps1 = sb("eps1", [128, 1], F32)
    invd = sb("invd", [128, 1], F32)
    nhalf = sb("nhalf", [128, 1], F32)
    lnv = sb("lnv", [128, NT], F32)

    xnf_r = xnf[:, :].rearrange("p (k c) -> p k c", c=CW)
    Pp_r = Pp[:, :].rearrange("p (k c) -> p k c", c=448)

    sem = lambda n: Sem(nc, st, n)
    d_cst = sem("d_cst")
    d_sml = sem("d_sml")
    d_vb = sem("d_vb")
    d_x = [sem("d_x0"), sem("d_x1")]
    d_pw = sem("d_pw")
    d_wv1 = sem("d_wv1")
    d_wo1 = sem("d_wo1")
    d_fw1 = sem("d_fw1")
    d_fw2 = sem("d_fw2")
    d_fw3 = sem("d_fw3")
    d_a7 = sem("d_a7")
    d_icb = sem("d_icb")
    d_xre = [sem("d_xre0"), sem("d_xre1")]
    d_out = [sem("d_out0"), sem("d_out1")]
    c_sq = sem("c_sq")
    c_var = sem("c_var")
    c_sd = sem("c_sd")
    c_xn = sem("c_xn")
    c_tp = sem("c_tp")
    c_xnf = sem("c_xnf")
    c_pc = sem("c_pc")
    c_pce = sem("c_pce")
    c_bcq = sem("c_bcq")
    c_pre = sem("c_pre")
    c_proj = sem("c_proj")
    c_proje = sem("c_proje")
    c_scan4 = sem("c_scan4")
    c_tri = sem("c_tri")
    c_xrk = sem("c_xrk")
    c_k = sem("c_k")
    c_kr = sem("c_kr")
    c_ksq = sem("c_ksq")
    c_r = sem("c_r")
    c_sig = sem("c_sig")
    c_v = sem("c_v")
    c_at = sem("c_at")
    c_e12 = sem("c_e12")
    c_pool = sem("c_pool")
    c_bc0 = sem("c_bc0")

    c_ms = sem("c_ms")
    nc.gpsimd.memset(ones1[:, :], 1.0)
    nc.gpsimd.memset(onesr[:, :], 1.0)
    for _k in range(KC):
        nc.gpsimd.memset(xnf[:, CW * _k:CW * _k + 2], 0.0)
        nc.gpsimd.memset(xnf[:, CW * _k + 2050:CW * (_k + 1)], 0.0)
    nc.gpsimd.memset(eps1[:, :], 1e-5)
    nc.gpsimd.memset(invd[:, :], 1.0 / D)
    c_ms.inc(nc.gpsimd.memset(nhalf[:, :], -1.0))
    g = nc.gpsimd

    x_tiles_loaded = []
    def _xdma(t):
        if t >= 4:
            nc.sync.wait_ge(c_xn.h, t - 3)
        d_x[t % 2].inc(nc.sync.dma_start(xin[t % 4][:, :],
                                         x_d[128 * t:128 * (t + 1), :]), 16)
        x_tiles_loaded.append(d_x[t % 2].n)
    _xdma(0)
    _xdma(1)
    d_cst.inc(nc.sync.dma_start(id_sb[:, :], ident_d), 16)
    d_cst.inc(nc.sync.dma_start(ictile[:, :], ictile_d), 16)
    d_cst.inc(nc.sync.dma_start(mus[:, :], mus_d), 16)
    d_cst.inc(nc.sync.dma_start(biasv[:, :], biasv_d), 16)
    d_cst.inc(nc.sync.dma_start(ab[:, :], ab_d), 16)
    for t in range(2, NT):
        _xdma(t)
    d_icb.inc(nc.sync.dma_start(icbc[0:64, :], icbc_d), 16)

    st1o = ExitStack()
    sb1 = lambda n, s, d_: st1o.enter_context(nc.sbuf_tensor(n, s, d_))
    PAr = sb1("PAr", [128, KC * 192], BF16)
    PBr = sb1("PBr", [128, KC * 192], BF16)
    sqd = sb1("sqd", [128, D], BF16)
    nc.gpsimd.memset(sqd[:, :], 0.0)
    xcb = sb1("xcb", [128, D], F32)
    UU = sb1("UU", [128, KC * 128], BF16)
    uts = sb1("uts", [128, KC * 64], BF16)
    st1b = ExitStack()
    wq_s = st1b.enter_context(nc.sbuf_tensor("wq_s", [128, KC * D], BF16))
    wk_s = st1b.enter_context(nc.sbuf_tensor("wk_s", [128, KC * D], BF16))
    st1a = ExitStack()
    winv_s = st1a.enter_context(nc.sbuf_tensor("winv_s", [128, KC * D], BF16))
    vbwb = st1a.enter_context(nc.sbuf_tensor("vbwb", [128, KC * 128], BF16))
    PAr_r = PAr[:, :].rearrange("p (k c) -> p k c", c=192)
    PBr_r = PBr[:, :].rearrange("p (k c) -> p k c", c=192)
    UU_r = UU[:, :].rearrange("p (k c) -> p k c", c=128)
    uts_r = uts[:, :].rearrange("p (k c) -> p k c", c=64)
    vbwb_r = vbwb[:, :].rearrange("p (k c) -> p k c", c=128)

    def chunked(dr):
        return dr.rearrange("(k p) m -> p k m", p=128)

    d_vb.inc(g.dma_start(vbwb_r[:, :, 0:64], chunked(vb_d)), 16)
    d_vb.inc(g.dma_start(vbwb_r[:, :, 64:128], chunked(wb_d)), 16)
    winv_r0 = winv_s[:, :].rearrange("p (k c) -> p k c", c=D)
    d_wv1.inc(g.dma_start(winv_r0[:, :, 0:512], chunked(winv_d)[:, :, 0:512]), 16)
    d_pw.inc(g.dma_start(winv_r0[:, :, 512:D], chunked(winv_d)[:, :, 512:D]), 16)
    d_sml.inc(g.dma_start(PAr_r[:, :, 0:64], chunked(vb_d)), 16)
    d_sml.inc(g.dma_start(PAr_r[:, :, 128:192], chunked(vt_d)), 16)
    d_sml.inc(g.dma_start(PBr_r[:, :, 0:64], chunked(wb_d)), 16)
    d_sml.inc(g.dma_start(PBr_r[:, :, 128:192], chunked(wt_d)), 16)
    d_pw.inc(g.dma_start(wq_s[:, :].rearrange("p (k c) -> p k c", c=D),
                         chunked(wq_d)), 16)
    d_pw.inc(g.dma_start(wk_s[:, :].rearrange("p (k c) -> p k c", c=D),
                         chunked(wk_d)), 16)

    psA = ExitStack()
    psTP = [psA.enter_context(nc.psum_tensor(f"psTP{i}", [128, 1024], BF16)) for i in range(3)]
    psPC = [psA.enter_context(nc.psum_tensor(f"psPC{i}", [128, 512], F32)) for i in range(2)]
    psPG = psA.enter_context(nc.psum_tensor("psPG", [128, 1024], F32))


    def emit_dve_ln_iter(it, extra=None):
        if it < NT:
            nc.vector.wait_ge(d_x[it % 2].h, x_tiles_loaded[it])
            nc.vector.reduce_sum(s1[:, it:it + 1], xin[it % 4][:, :],
                                 axis=mybir.AxisListType.X)
        elif it == NT:
            nc.vector.tensor_copy(sqd[:, 0:256], sqd[:, 0:256])
        if 1 <= it <= NT:
            t = it - 1
            nc.vector.wait_ge(c_sd.h, t + 1)
            nc.vector.reciprocal(rstdv[:, t:t + 1], lnv[:, t:t + 1])
        if 2 <= it <= NT + 1:
            t = it - 2
            nc.vector.wait_ge(c_tp.h, t + 1)
            srcv = psTP[t % 3][:, :].rearrange("p (k c) -> p k c", c=128)
            nc.vector.tensor_copy(
                xnf_r[:, :, 2 + 128 * t:2 + 128 * t + 64],
                srcv[:, :, 0:64])
        else:
            nc.vector.tensor_copy(sqd[:, 0:256], sqd[:, 0:256])
        if 1 <= it <= NT:
            t = it - 1
            c_xn.inc(nc.vector.tensor_scalar(xntm[t % 2][:, :], xin[t % 4][:, :],
                                             mean[:, t:t + 1], rstdv[:, t:t + 1],
                                             ALU.subtract, ALU.mult))
        else:
            nc.vector.tensor_copy(sqd[:, 0:256], sqd[:, 0:256])
        if it < NT:
            nc.vector.tensor_scalar(mean[:, it:it + 1], s1[:, it:it + 1],
                                    invd[:, 0:1], None, ALU.mult)
        if 2 <= it <= NT + 1:
            t = it - 2
            srcv = psTP[t % 3][:, :].rearrange("p (k c) -> p k c", c=128)
            c_xnf.inc(nc.vector.tensor_copy(
                xnf_r[:, :, 2 + 128 * t + 64:2 + 128 * (t + 1)],
                srcv[:, :, 64:128]))
        else:
            nc.vector.tensor_copy(sqd[:, 0:256], sqd[:, 0:256])
        if it < NT:
            c_var.inc(nc.vector.tensor_scalar(m2[:, it:it + 1], mean[:, it:it + 1],
                                              mean[:, it:it + 1], nhalf[:, 0:1],
                                              ALU.mult, ALU.mult))
        if extra is not None:
            extra()

    def emit_act_ln_iter(it, extra=None):
        if 1 <= it <= NT:
            t = it - 1
            nc.scalar.wait_ge(c_var.h, t + 1)
            nc.scalar.activation(var[:, t:t + 1], s2[:, t:t + 1], AF.Identity,
                                 scale=invd[:, 0:1], bias=m2[:, t:t + 1])
            c_sd.inc(nc.scalar.activation(lnv[:, t:t + 1], var[:, t:t + 1], AF.Sqrt,
                                          bias=eps1[:, 0:1]))
        if it < NT:
            nc.scalar.wait_ge(d_x[it % 2].h, x_tiles_loaded[it])
            if it == 0:
                nc.scalar.wait_ge(c_ms.h, 1)
            nc.scalar.activation(sqd[:, :], xin[it % 4][:, :], AF.Square,
                                 accum_out=s2[:, it:it + 1])
        if extra is not None:
            extra()

    def emit_pe_tp(t):
        if t == 0:
            nc.tensor.wait_ge(d_cst.h, d_cst.n)
            nc.tensor.wait_ge(c_ms.h, 1)
        nc.tensor.wait_ge(c_xn.h, t + 1)
        if t >= 3:
            nc.tensor.wait_ge(c_xnf.h, t - 2)
        for k in range(KC):
            inst = nc.tensor.transpose(psTP[t % 3][:, 128 * k:128 * (k + 1)],
                                       xntm[t % 2][:, 128 * k:128 * (k + 1)], id_sb[:, :])
            if k == KC - 1:
                c_tp.inc(inst)

    winv_r = winv_s[:, :].rearrange("p (k c) -> p k c", c=D)
    wqr_box = {}

    def emit_tmp12(m):
        if m == 0:
            nc.tensor.wait_ge(d_wv1.h, 16)
            nc.tensor.wait_ge(d_vb.h, 32)
        if m == 4:
            nc.tensor.wait_ge(d_pw.h, 16)
        if m >= 2:
            nc.tensor.wait_ge(c_pce.h, 2 * (m - 1))
        for k in range(KC):
            inst = nc.tensor.matmul(psPC[m % 2][:, 0:128],
                                    winv_r[:, k, 128 * m:128 * (m + 1)],
                                    vbwb[:, 128 * k:128 * (k + 1)],
                                    start=(k == 0), stop=(k == KC - 1))
        c_pc.inc(inst)

    def emit_pa(m):
        if m == 0:
            nc.tensor.wait_ge(d_pw.h, 32)
            nc.tensor.wait_ge(d_sml.h, 64)
            nc.tensor.wait_ge(c_pce.h, 16)
        if m >= 2:
            nc.tensor.wait_ge(c_pce.h, 16 + 2 * (m - 1))
        for k in range(KC):
            inst = nc.tensor.matmul(psPC[m % 2][:, 0:192],
                                    wqr_box["wq"][:, k, 128 * m:128 * (m + 1)],
                                    PAr[:, 192 * k:192 * (k + 1)],
                                    start=(k == 0), stop=(k == KC - 1))
        c_pc.inc(inst)

    def emit_pb(m):
        if m == 0:
            nc.tensor.wait_ge(d_pw.h, 48)
        nc.tensor.wait_ge(c_pce.h, 30 + 2 * m)
        for k in range(KC):
            inst = nc.tensor.matmul(psPC[m % 2][:, 0:192],
                                    wqr_box["wk"][:, k, 128 * m:128 * (m + 1)],
                                    PBr[:, 192 * k:192 * (k + 1)],
                                    start=(k == 0), stop=(k == KC - 1))
        c_pc.inc(inst)

    pe_slots = {2: [0, 1], 3: [2, 3], 4: [4, 5], 5: [6, 7],
                11: [8, 9], 12: [10, 11], 13: [12, 13], 14: [14, 15],
                15: [16, 17]}
    pre_units = ([lambda m=m: emit_tmp12(m) for m in range(KC)]
                 + [lambda m=m: emit_pa(m) for m in range(KC)]
                 + [lambda m=m: emit_pb(m) for m in range(KC)])

    for t in range(NT):
        emit_pe_tp(t)
        if stage >= 2:
            for u in pe_slots.get(t, []):
                pre_units[u]()
            if t == 8:
                st1a.close()
                woT_s = st1b.enter_context(nc.sbuf_tensor("woT_s", [128, KC * D], BF16))
                d_sml.inc(g.dma_start(UU_r[:, :, 0:64], chunked(ub_d)), 16)
                d_sml.inc(g.dma_start(UU_r[:, :, 64:128], chunked(ub_d)), 16)
                d_sml.inc(g.dma_start(uts_r[:, :, :], chunked(ut_d)), 16)
                d_sml.inc(g.dma_start(Pp_r[:, :, 320:384], chunked(xt_d)), 16)
                g.wait_ge(c_pc.h, 8)
                woT_rr = woT_s[:, :].rearrange("p (k c) -> p k c", c=D)
                woT_dr = woT_d.rearrange("(k p) m -> p k m", p=128)
                d_wo1.inc(g.dma_start(woT_rr[:, :, 0:512], woT_dr[:, :, 0:512]), 16)
                d_pw.inc(g.dma_start(woT_rr[:, :, 512:D], woT_dr[:, :, 512:D]), 16)
                g.wait_ge(d_x[0].h, 16 * (NT // 2))
                g.wait_ge(d_x[1].h, 16 * (NT // 2))
                d_fw1.inc(g.dma_start(wkT[:, :].rearrange("p (k c) -> p k c", c=D),
                                      wkT_d.rearrange("(k p) m -> p k m", p=128)), 16)
                d_fw2.inc(g.dma_start(wrT[:, :].rearrange("p (k c) -> p k c", c=D),
                                      wrT_d.rearrange("(k p) m -> p k m", p=128)), 16)
                d_fw3.inc(g.dma_start(wvT[:, :].rearrange("p (k c) -> p k c", c=D),
                                      wvT_d.rearrange("(k p) m -> p k m", p=128)), 16)
                wqr_box["wq"] = wq_s[:, :].rearrange("p (k c) -> p k c", c=D)
                wqr_box["wk"] = wk_s[:, :].rearrange("p (k c) -> p k c", c=D)
                wqr_box["woT"] = woT_s[:, :].rearrange("p (k c) -> p k c", c=D)

    if stage < 2:
        for it in range(NT + 2):
            emit_dve_ln_iter(it)
        _dbg_finish(nc, g, stage, locals())
        st1a.close(); st1b.close(); st1o.close(); psA.close(); st.close()
        return nc
    for u in range(18, 24):
        pre_units[u]()

    woT_r = wqr_box["woT"]
    nc.tensor.wait_ge(d_wo1.h, 16)
    nc.tensor.wait_ge(c_bcq.h, 1)
    for h in range(2):
        if h == 1:
            nc.tensor.wait_ge(d_pw.h, 64)
        for k in range(KC):
            inst = nc.tensor.matmul(psPG[0:1, 512 * h:512 * (h + 1)],
                                    bcq[:, k:k + 1], woT_r[:, k, 512 * h:512 * (h + 1)],
                                    start=(k == 0), stop=(k == KC - 1))
    c_pc.inc(inst)
    nc.tensor.wait_ge(c_pce.h, 49)
    nc.tensor.wait_ge(d_sml.h, 96)
    for h in range(2):
        for k in range(KC):
            inst = nc.tensor.matmul(psPG[:, 512 * h:512 * (h + 1)],
                                    UU[:, 128 * k:128 * (k + 1)],
                                    woT_r[:, k, 512 * h:512 * (h + 1)],
                                    start=(k == 0), stop=(k == KC - 1))
    c_pc.inc(inst)
    for h in range(2):
        inst = nc.tensor.matmul(psPC[h][:, :], onesr[:, :],
                                c0row[:, 512 * h:512 * (h + 1)], start=True, stop=True)
    c_pc.inc(inst)
    nc.tensor.wait_ge(c_pce.h, 50)
    nc.tensor.wait_ge(d_sml.h, 112)
    for h in range(2):
        for k in range(KC):
            inst = nc.tensor.matmul(psPG[0:64, 512 * h:512 * (h + 1)],
                                    uts[:, 64 * k:64 * (k + 1)],
                                    woT_r[:, k, 512 * h:512 * (h + 1)],
                                    start=(k == 0), stop=(k == KC - 1))
    c_pc.inc(inst)

    def act_tmp12_evac(m):
        nc.scalar.wait_ge(c_pc.h, m + 1)
        c_pce.inc(nc.scalar.activation(PAr_r[:, m, 64:128], psPC[m % 2][:, 64:128],
                                       AF.Identity, scale=ab[:, 0:1]))
        c_pce.inc(nc.scalar.copy(PBr_r[:, m, 64:128], psPC[m % 2][:, 0:64]))

    def act_pa_evac(m):
        nc.scalar.wait_ge(c_pc.h, 9 + m)
        c_pce.inc(nc.scalar.copy(Pp_r[:, m, 0:128], psPC[m % 2][:, 0:128]))
        c_pce.inc(nc.scalar.copy(Pp_r[:, m, 256:320], psPC[m % 2][:, 128:192]))

    def act_pb_evac(m):
        nc.scalar.wait_ge(c_pc.h, 17 + m)
        c_pce.inc(nc.scalar.copy(Pp_r[:, m, 128:256], psPC[m % 2][:, 0:128]))
        c_pce.inc(nc.scalar.copy(Pp_r[:, m, 384:448], psPC[m % 2][:, 128:192]))

    act_slots = {4: [0, 1], 5: [2, 3], 6: [4, 5], 7: [6, 7],
                 14: [8, 9], 15: [10, 11], 16: [12, 13], 17: [14, 15]}
    act_units = ([lambda m=m: act_tmp12_evac(m) for m in range(KC)]
                 + [lambda m=m: act_pa_evac(m) for m in range(KC)]
                 + [lambda m=m: act_pb_evac(m) for m in range(KC)])

    for it in range(NT + 2):
        def _extra(it=it):
            for u in act_slots.get(it, []):
                act_units[u]()
        emit_act_ln_iter(it, _extra)
    for u in range(16, 24):
        act_units[u]()
    nc.scalar.wait_ge(c_pc.h, 25)
    c_pce.inc(nc.scalar.copy(c0row[:, :], psPG[0:1, :]))
    nc.scalar.wait_ge(c_pc.h, 26)
    c_pce.inc(nc.scalar.copy(Pg[:, :], psPG[:, :]))
    nc.scalar.wait_ge(c_pc.h, 27)
    c_pce.inc(nc.scalar.copy(c0b[:, 0:512], psPC[0][:, :]))
    c_pce.inc(nc.scalar.copy(c0b[:, 512:1024], psPC[1][:, :]))
    nc.scalar.wait_ge(c_pc.h, 28)
    c_pce.inc(nc.scalar.activation(Pgt[0:64, :], psPG[0:64, :], AF.Identity,
                                   scale=ab[0:64, 1:2]))

    def _dve_extra(it):
        if it == 1:
            nc.vector.wait_ge(d_cst.h, d_cst.n)
            nc.vector.scalar_tensor_tensor(bc32[:, :], biasv[:, 0:8], ab[:, 0:1],
                                           biasv[:, 0:8], ALU.mult, ALU.add)
            nc.vector.tensor_scalar(bct[:, :], biasv[:, 8:16], ab[:, 1:2],
                                    None, ALU.mult)
        elif it == 2:
            nc.vector.tensor_tensor(bc32[:, :], bc32[:, :], bct[:, :], ALU.add)
        elif it == 4:
            c_bcq.inc(nc.vector.tensor_copy(bcq[:, :], bc32[:, :]))
    for it in range(NT + 2):
        emit_dve_ln_iter(it, lambda it=it: _dve_extra(it))

    st1b.close()
    st1o.close()
    psA.close()

    st2 = ExitStack()
    sb2 = lambda n, s, d_: st2.enter_context(nc.sbuf_tensor(n, s, d_))
    xrb = [sb2(f"xrb{i}", [128, KC * 512], BF16) for i in range(2)]
    xkb = [sb2(f"xkb{i}", [128, KC * 512], BF16) for i in range(2)]
    kact = sb2("kact", [128, KC * 512], BF16)
    dbuf = sb2("dbuf", [128, 512], BF16)
    kr = [sb2(f"kr{i}", [128, 512], BF16) for i in range(4)]
    r_tm = [sb2(f"r_tm{i}", [128, D], BF16) for i in range(2)]
    e1 = sb2("e1", [128, D], F32)
    e2 = sb2("e2", [128, D], F32)
    xre = [sb2(f"xre{i}", [128, D], F32) for i in range(2)]

    if stage < 3:
        _dbg_finish(nc, g, stage, locals())
        st2.close(); st.close()
        return nc
    psVA = ExitStack()
    psV = psVA.enter_context(nc.psum_tensor("psV", [128, 1024], F32))
    psAT = psVA.enter_context(nc.psum_tensor("psAT", [128, 1024], F32))
    psB = ExitStack()
    pj = [psB.enter_context(nc.psum_tensor(f"pj{p}", [128, 512], F32))
          for p in range(4)]

    nc.tensor.wait_ge(c_xnf.h, NT)
    nc.tensor.wait_ge(c_pce.h, 53)
    nc.tensor.wait_ge(d_sml.h, 128)
    proj_groups = []
    for s in range(NS):
        for p in range(4):
            if s >= 1:
                nc.tensor.wait_ge(c_proje.h, 4 * (s - 1) + p + 1)
            dst = pj[p][0:64, :] if p == 3 else pj[p][:, :]
            for k in range(KC):
                lcol = 448 * k + (384 if p == 3 else 128 * p)
                lw = 64 if p == 3 else 128
                inst = nc.tensor.matmul(dst, Pp[:, lcol:lcol + lw],
                                        xnf_r[:, k, 2 + 512 * s:2 + 512 * (s + 1)],
                                        start=(k == 0), stop=(k == KC - 1))
            c_proj.inc(inst)
            proj_groups.append((s, p, c_proj.n))

    evac_count = {}
    _ec = 0
    for s, p, pn in proj_groups:
        nc.scalar.wait_ge(c_proj.h, pn)
        if p == 0 or p == 1:
            dst = SA if p == 0 else SB
            c_proje.inc(nc.scalar.copy(dst[:, 512 * s:512 * (s + 1)],
                                       pj[p][:, :]))
            _ec += 1
        elif p == 2:
            nc.scalar.copy(SC[0:64, 512 * s:512 * (s + 1)], pj[2][0:64, :])
            c_proje.inc(nc.scalar.copy(SD[64:128, 512 * s:512 * (s + 1)],
                                       pj[2][64:128, :]))
            c_proje.n += 0
            _ec += 1
            c_proje.inc(nc.scalar.engine_nop()) if False else None
            _ec += 0
        else:
            c_proje.inc(nc.scalar.copy(SD[0:64, 512 * s:512 * (s + 1)],
                                       pj[3][0:64, :]))
            _ec += 1
        evac_count[(s, p)] = c_proje.n

    if stage < 4:
        _dbg_finish(nc, g, stage, locals())
        psB.close(); psVA.close(); st2.close(); st.close()
        return nc
    psB.close()
    psC = ExitStack()
    psK = [psC.enter_context(nc.psum_tensor(f"psK{i}", [128, 512], F32)) for i in range(2)]
    psR = psC.enter_context(nc.psum_tensor("psR", [128, 1024], F32))

    def emit_xrk(s, wait_r=None):
        nc.vector.wait_ge(c_pc.h, 28)
        if wait_r is not None:
            nc.vector.wait_ge(c_r.h, wait_r)
        for c in range(KC):
            xc = xnf_r[:, c, 2 + 512 * s:2 + 512 * (s + 1)]
            xp = xnf_r[:, c, 1 + 512 * s:1 + 512 * (s + 1)]
            nc.vector.tensor_tensor(dbuf[:, :], xc, xp, ALU.subtract)
            nc.vector.scalar_tensor_tensor(xrb[s % 2][:, 512 * c:512 * (c + 1)],
                                           dbuf[:, :], mus[:, c:c + 1], xp,
                                           ALU.mult, ALU.add)
            inst = nc.vector.scalar_tensor_tensor(xkb[s % 2][:, 512 * c:512 * (c + 1)],
                                                  dbuf[:, :], mus[:, 8 + c:9 + c], xp,
                                                  ALU.mult, ALU.add)
        c_xrk.inc(inst)

    emit_xrk(0)

    H = T // 2
    ones_h = ones1[:, 0:1].broadcast_to((128, H))
    nc.vector.wait_ge(c_ms.h, 1)
    nc.vector.wait_ge(c_proje.h, 4 * 1 + 2)
    nc.vector.tensor_tensor_scan(psV[:, :], ones_h, SB[:, 0:H], 0.0, ALU.mult, ALU.add)
    nc.vector.tensor_tensor(SB[:, 0:H], SA[:, 0:H], psV[:, :], ALU.mult)
    nc.vector.wait_ge(c_proje.h, 4 * 3 + 2)
    nc.vector.tensor_tensor_scan(psAT[:, :], ones_h, SB[:, H:T],
                                 psV[:, H - 1:H], ALU.mult, ALU.add)
    nc.vector.tensor_tensor(SB[:, H:T], SA[:, H:T], psAT[:, :], ALU.mult)
    nc.vector.wait_ge(c_proje.h, 4 * 1 + 4)
    nc.vector.tensor_tensor_scan(psV[:, :], ones_h, SD[:, 0:H], 0.0, ALU.mult, ALU.add)
    nc.vector.tensor_copy(SC[64:128, 0:H], psV[64:128, :])
    nc.vector.wait_ge(c_proje.h, 16)
    nc.vector.tensor_tensor_scan(psAT[:, :], ones_h, SD[:, H:T],
                                 psV[:, H - 1:H], ALU.mult, ALU.add)
    c_scan4.inc(nc.vector.tensor_copy(SC[64:128, H:T], psAT[64:128, :]))
    g.wait_ge(c_scan4.h, 1)
    d_a7.inc(g.dma_start(SA[0:64, :], SC[64:128, :]), 16)
    nc.vector.tensor_tensor(SD[0:64, 0:H], SC[0:64, 0:H], psV[0:64, :], ALU.mult)
    nc.vector.tensor_tensor(SD[0:64, H:T], SC[0:64, H:T], psAT[0:64, :], ALU.mult)
    nc.vector.wait_ge(d_a7.h, 16)
    nc.vector.wait_ge(d_icb.h, 16)
    nc.vector.tensor_tensor(SC[0:64, :], SA[0:64, :], icbc[0:64, :], ALU.mult)
    c_tri.inc(nc.vector.tensor_tensor(SD[0:64, :], SD[0:64, :], SC[0:64, :], ALU.mult))
    emit_xrk(1)

    if stage < 5:
        _dbg_finish(nc, g, stage, locals())
        psC.close(); psVA.close(); st2.close(); st.close()
        return nc
    nc.tensor.wait_ge(c_proje.h, 16)
    nc.tensor.wait_ge(d_fw1.h, 16)
    for s in range(NS):
        nc.tensor.wait_ge(c_xrk.h, s + 1)
        for d8 in range(KC):
            gi = KC * s + d8
            if gi >= 2:
                nc.tensor.wait_ge(c_kr.h, gi - 1)
            for c in range(KC):
                inst = nc.tensor.matmul(psK[d8 % 2][:, :],
                                        wkT[:, 1024 * c + 128 * d8:1024 * c + 128 * (d8 + 1)],
                                        xkb[s % 2][:, 512 * c:512 * (c + 1)],
                                        start=(c == 0), stop=(c == KC - 1))
            c_k.inc(inst)
        if s == 0:
            nc.tensor.wait_ge(d_fw2.h, 16)
            nc.tensor.wait_ge(d_fw3.h, 16)
        for j in range(4):
            t = 4 * s + j
            if t >= 1:
                nc.tensor.wait_ge(c_sig.h, t)
            for c in range(KC):
                lhsT = xrb[s % 2][:, 512 * c + 128 * j:512 * c + 128 * (j + 1)]
                nc.tensor.matmul(psR[:, 0:512], lhsT, wrT[:, 1024 * c:1024 * c + 512],
                                 start=(c == 0), stop=(c == KC - 1))
                inst = nc.tensor.matmul(psR[:, 512:1024], lhsT,
                                        wrT[:, 1024 * c + 512:1024 * (c + 1)],
                                        start=(c == 0), stop=(c == KC - 1))
            c_r.inc(inst)
            if s == 0 and j == 0:
                nc.tensor.wait_ge(c_tri.h, 1)
            if j == 0:
                nc.tensor.wait_ge(c_ksq.h, KC * (s + 1))
            if t >= 1:
                nc.tensor.wait_ge(c_e12.h, 2 * t)
            for d8 in range(KC):
                lhsT = kact[:, 512 * d8 + 128 * j:512 * d8 + 128 * (j + 1)]
                nc.tensor.matmul(psV[:, 0:512], lhsT, wvT[:, 1024 * d8:1024 * d8 + 512],
                                 start=(d8 == 0), stop=(d8 == KC - 1))
                inst = nc.tensor.matmul(psV[:, 512:1024], lhsT,
                                        wvT[:, 1024 * d8 + 512:1024 * (d8 + 1)],
                                        start=(d8 == 0), stop=(d8 == KC - 1))
            c_v.inc(inst)
            for h in range(2):
                nc.tensor.matmul(psAT[:, 512 * h:512 * (h + 1)],
                                 SB[:, 128 * t:128 * (t + 1)],
                                 Pg[:, 512 * h:512 * (h + 1)], start=True, stop=False)
                inst = nc.tensor.matmul(psAT[:, 512 * h:512 * (h + 1)],
                                        SD[0:64, 128 * t:128 * (t + 1)],
                                        Pgt[0:64, 512 * h:512 * (h + 1)],
                                        start=False, stop=True)
            c_at.inc(inst)

    for s in range(NS):
        for d8 in range(KC):
            gi = KC * s + d8
            nc.scalar.wait_ge(c_k.h, gi + 1)
            c_kr.inc(nc.scalar.activation(kr[gi % 4][:, :], psK[d8 % 2][:, :], AF.Relu))
            c_ksq.inc(nc.scalar.activation(kact[:, 512 * d8:512 * (d8 + 1)],
                                           kr[gi % 4][:, :], AF.Square))
        for j in range(4):
            t = 4 * s + j
            nc.scalar.wait_ge(c_r.h, t + 1)
            if t >= 2:
                nc.scalar.wait_ge(c_e12.h, 2 * t - 2)
            c_sig.inc(nc.scalar.activation(r_tm[j % 2][:, :], psR[:, :], AF.Sigmoid))

    xre_loaded_box = {t: 16 * (t // 2 + 1) for t in range(NT)}
    def emit_e(s, j):
        t = 4 * s + j
        nc.vector.wait_ge(c_v.h, t + 1)
        nc.vector.wait_ge(c_sig.h, t + 1)
        c_e12.inc(nc.vector.tensor_tensor(e2[:, :], r_tm[j % 2][:, :], psV[:, :],
                                          ALU.mult))
        nc.vector.wait_ge(c_at.h, t + 1)
        c_e12.inc(nc.vector.scalar_tensor_tensor(e1[:, :], psAT[:, :],
                                                 ictile[:, t:t + 1], c0b[:, :],
                                                 ALU.mult, ALU.add))
        nc.vector.wait_ge(d_xre[t % 2].h, xre_loaded_box[t])
        if t >= 2:
            nc.vector.wait_ge(d_out[t % 2].h, 16 * (t // 2))
        nc.vector.tensor_tensor(e1[:, :], e1[:, :], e2[:, :], ALU.add)
        c_pool.inc(nc.vector.tensor_tensor(xre[t % 2][:, :], e1[:, :], xre[t % 2][:, :],
                                           ALU.add))
    for s in range(NS):
        for j in range(4):
            emit_e(s, j)
        if s + 2 < NS:
            emit_xrk(s + 2, wait_r=4 * (s + 1))

    nc.sync.wait_ge(c_pce.h, 53)
    xre_loaded = []
    for t in range(2):
        d_xre[t % 2].inc(nc.sync.dma_start(xre[t % 2][:, :],
                                           x_d[128 * t:128 * (t + 1), :]), 16)
        xre_loaded.append(d_xre[t % 2].n)
    for t in range(NT):
        nc.sync.wait_ge(c_pool.h, t + 1)
        d_out[t % 2].inc(nc.sync.dma_start(out_d[128 * t:128 * (t + 1), :],
                                           xre[t % 2][:, :]), 16)
        if t + 2 < NT:
            d_xre[t % 2].inc(nc.sync.dma_start(xre[t % 2][:, :],
                                               x_d[128 * (t + 2):128 * (t + 3), :]), 16)
            xre_loaded.append(d_xre[t % 2].n)

    nc.sync.wait_ge(d_out[0].h, 16 * (NT // 2))
    nc.sync.wait_ge(d_out[1].h, 16 * (NT // 2))

    psC.close()
    psVA.close()
    st2.close()
    st.close()
    return nc


def _dbg_finish(nc, g, stage, L):
    import concourse.mybir as mybir
    F32 = mybir.dt.float32
    d_dbg = L["st"].enter_context(nc.semaphore("d_dbg"))
    n = 0
    def dump(name, ap):
        nonlocal n
        o = nc.dram_tensor("dbg_" + name, list(ap.shape), F32, kind="ExternalOutput").ap()
        g.dma_start(o, ap).then_inc(d_dbg, 16)
        n += 16
    if stage >= 1:
        g.wait_ge(L["c_xnf"].h, 16)
        dump("xnf", L["xnf"][:, :])
    if stage >= 2:
        g.wait_ge(L["c_pre"].h, 1)
        dump("Pp", L["Pp"][:, :])
        dump("Pg", L["Pg"][:, :])
        dump("Pgt", L["Pgt"][0:64, :])
        dump("c0b", L["c0b"][:, :])
    if stage >= 3:
        g.wait_ge(L["c_proje"].h, 16)
        dump("SA", L["SA"][:, :])
        dump("SB", L["SB"][:, :])
        dump("SC", L["SC"][:, :])
        dump("SD", L["SD"][0:64, :])
    if stage >= 4:
        g.wait_ge(L["c_tri"].h, 1)
        dump("G", L["SB"][:, :])
        dump("TR", L["SD"][0:64, :])
    g.wait_ge(d_dbg, n)


_NC_CACHE = None


def kernel(**inputs):
    global _NC_CACHE
    f32 = lambda a: np.ascontiguousarray(np.asarray(a, dtype=np.float32))
    x = f32(inputs["L0_attn_x"])
    chunkT = lambda v: np.ascontiguousarray(f32(v).reshape(KC, 128).T)

    shared = dict(
        wq=f32(inputs["L0_attn_WQ"]),
        wk=f32(inputs["L0_attn_WK"]),
        winv=f32(inputs["L0_attn_Winv"]),
        woT=np.ascontiguousarray(f32(inputs["L0_attn_WO"]).T),
        wrT=np.ascontiguousarray(f32(inputs["L0_ffn_W_r"]).T),
        wkT=np.ascontiguousarray(f32(inputs["L0_ffn_W_k"]).T),
        wvT=np.ascontiguousarray(f32(inputs["L0_ffn_W_v"]).T),
        vb=f32(inputs["L0_attn_V_b"]),
        wb=f32(inputs["L0_attn_W_b"]),
        vt=f32(inputs["L0_attn_V_t"]),
        wt=f32(inputs["L0_attn_W_t"]),
        xt=f32(inputs["L0_attn_X_t"]),
        ub=f32(inputs["L0_attn_U_b"]),
        ut=f32(inputs["L0_attn_U_t"]),
        biasv=np.ascontiguousarray(
            np.concatenate([chunkT(inputs["L0_attn_bias_b"]),
                            chunkT(inputs["L0_attn_bias_t"])], axis=1)),
        mus=np.ascontiguousarray(
            np.concatenate([chunkT(inputs["L0_ffn_mu_r"]),
                            chunkT(inputs["L0_ffn_mu_k"])], axis=1)),
        ab=np.ascontiguousarray(np.broadcast_to(
            np.stack([np.float32(inputs["L0_attn_alpha_bi"]),
                      np.float32(inputs["L0_attn_alpha_tri"])]).astype(np.float32),
            (128, 2))),
        ident=np.eye(128, dtype=BFD),
        icbc=np.ascontiguousarray(np.broadcast_to(
            1.0 / np.arange(1, T + 1, dtype=np.float32), (64, T))),
        ictile=np.ascontiguousarray(
            (1.0 / (1.0 + np.arange(T, dtype=np.float32).reshape(NT, 128).T))),
    )

    if _NC_CACHE is None:
        _NC_CACHE = build_nc()
    in_maps = [dict(shared, x=np.ascontiguousarray(x[b])) for b in range(8)]
    res = run_bass_kernel_spmd(_NC_CACHE, in_maps, core_ids=list(range(8)))
    out = np.stack([np.asarray(res.results[b]["out"], dtype=np.float32)
                    for b in range(8)])
    return out
